# version 03
# baseline (speedup 1.0000x reference)
import numpy as np
import ml_dtypes
from contextlib import ExitStack

import concourse.bass as bass
import concourse.mybir as mybir
from concourse.bass_utils import run_bass_kernel_spmd

F32 = mybir.dt.float32
BF16 = mybir.dt.bfloat16
I32 = mybir.dt.int32
AF = mybir.ActivationFunctionType
ALU = mybir.AluOpType

PI_F = float(np.float32(np.pi))
TWO_PI_HI = float(np.float32(2 * np.pi))
TWO_PI_LO = float(np.float64(2 * np.pi) - np.float64(np.float32(2 * np.pi)))
INV_2PI = float(1.0 / (2 * np.pi))
EXP_BIAS = -24.0
SM_SCALE = 0.03125

SEQ = 4096
DM = 1024
DFF = 2816
BATCH = 4


def halves_block_pattern(nblk):
    A = [j for j in range(nblk) if j % 4 in (0, 3)]
    B = [j for j in range(nblk) if j % 4 in (1, 2)]
    return sorted(A), sorted(B)


def build_program(seq=SEQ, d=DM, dff=DFF, phases=9, p0b_jobs=None, p1b_mode=9, dbg=False):
    nc = bass.Bass("TRN2", target_bir_lowering=False, detect_race_conditions=False)
    es = ExitStack()

    nkt = seq // 128
    own = seq // 2
    nqb = own // 256
    ntile_own = own // 128
    ndc = d // 128
    npair = ndc // 2
    nicc = dff // 128
    iw = [512] * (dff // 512) + ([dff % 512] if dff % 512 else [])
    nblkseq = seq // 512
    nqblk = own // 512

    def sb(name, shape, dt=F32, stack=None):
        return (stack or es).enter_context(nc.sbuf_tensor(name, shape, dt))

    def ps_(name, shape=(128, 512), dt=F32):
        return es.enter_context(nc.psum_tensor(name, list(shape), dt))

    class Sem:
        def __init__(self, name):
            self.h = es.enter_context(nc.semaphore(name))
            self.n = 0

        def inc(self, k=1):
            self.n += k
            return (self, self.n)

        @property
        def last(self):
            return (self, self.n)

    def W(eng, *evs):
        for ev in evs:
            if ev is not None and ev[1] > 0:
                eng.wait_ge(ev[0].h, ev[1])

    x = nc.dram_tensor("x", [seq, d], F32, kind="ExternalInput").ap()
    xq_own = nc.dram_tensor("xq_own", [own, d], F32, kind="ExternalInput").ap()
    pos_own = nc.dram_tensor("pos_own", [128, own], F32, kind="ExternalInput").ap()
    mbnd = nc.dram_tensor("mbnd", [128, nqb * 4], F32, kind="ExternalInput").ap()
    wq = nc.dram_tensor("wq", [d, d], F32, kind="ExternalInput").ap()
    wk = nc.dram_tensor("wk", [d, d], F32, kind="ExternalInput").ap()
    wv = nc.dram_tensor("wv", [d, d], F32, kind="ExternalInput").ap()
    wg = nc.dram_tensor("wg", [dff, d], F32, kind="ExternalInput").ap()
    wu = nc.dram_tensor("wu", [dff, d], F32, kind="ExternalInput").ap()
    wd = nc.dram_tensor("wd", [d, dff], F32, kind="ExternalInput").ap()
    y = nc.dram_tensor("y", [own, d], F32, kind="ExternalOutput").ap()

    knd = "ExternalOutput" if dbg else "Internal"
    if dbg:
        dbg_xn = nc.dram_tensor("dbg_xn", [128, d], BF16, kind="ExternalOutput").ap()
        dbg_cos = nc.dram_tensor("dbg_cos", [128, 512], BF16, kind="ExternalOutput").ap()
        dbg_sin = nc.dram_tensor("dbg_sin", [128, 512], BF16, kind="ExternalOutput").ap()
        dbg_xnT = nc.dram_tensor("dbg_xnT", [128, 8 * 512], BF16, kind="ExternalOutput").ap()
        dbg_xnq = nc.dram_tensor("dbg_xnq", [128, d], F32, kind="ExternalOutput").ap()
        dbg_xqq = nc.dram_tensor("dbg_xqq", [128, d], BF16, kind="ExternalOutput").ap()
        dbg_gate = nc.dram_tensor("dbg_gate", [128, 512], F32, kind="ExternalOutput").ap()
        dbg_hq = nc.dram_tensor("dbg_hq", [128, dff], BF16, kind="ExternalOutput").ap()
        dbg_st = nc.dram_tensor("dbg_st", [128, 64], F32, kind="ExternalOutput").ap()
        dbg_xf = nc.dram_tensor("dbg_xf", [128, d], F32, kind="ExternalOutput").ap()
        dbg_sil = nc.dram_tensor("dbg_sil", [128, 512], F32, kind="ExternalOutput").ap()
        dbg_up = nc.dram_tensor("dbg_up", [128, 512], F32, kind="ExternalOutput").ap()
        dbg_hb = nc.dram_tensor("dbg_hb", [128, 512], F32, kind="ExternalOutput").ap()
        dbg_sb2 = nc.dram_tensor("dbg_sb2", [128, 512], F32, kind="ExternalOutput").ap()
        dbg_bm2 = nc.dram_tensor("dbg_bm2", [128, 64], F32, kind="ExternalOutput").ap()
    qt_st = nc.dram_tensor("qt_st", [d, own], BF16, kind=knd).ap()
    at_st = nc.dram_tensor("at_st", [own, d], BF16, kind=knd).ap()
    wgT_st = nc.dram_tensor("wgT_st", [d, dff], BF16, kind=knd).ap()
    wuT_st = nc.dram_tensor("wuT_st", [d, dff], BF16, kind=knd).ap()
    wdT_st = nc.dram_tensor("wdT_st", [dff, d], BF16, kind=knd).ap()

    s_x = Sem("s_x")
    s_hw = Sem("s_hw")
    s_sc = Sem("s_sc")
    s_gp = Sem("s_gp")
    s_v = Sem("s_v")
    s_a = Sem("s_a")
    s_t = Sem("s_t")
    s_g = Sem("s_g")

    blk = es.enter_context(nc.Block())

    consts = sb("consts", [128, 8], F32)
    iv = sb("iv", [128, npair], F32)
    ones_bf = sb("ones_bf", [128, 1], BF16)
    iota_d = sb("iota_d", [128, 256], F32)
    mb_bc = sb("mb_bc", [128, nqb * 4], F32)
    statsq = sb("statsq", [128, ntile_own * 2], F32)
    six_b = sb("six_b", [128, 1], F32)

    pb = [ps_(f"pb{i}") for i in range(8)]

    scr_i = sb("scr_i", [128, 256], I32)
    scr_f2 = sb("scr_f2", [128, 8], F32)
    scr_p = sb("scr_p", [128, 1], I32)
    lnb = sb("lnb", [128, 1], F32)

    @blk.sync
    def _(sy):
        sy.dma_start(out=mb_bc[:], in_=mbnd).then_inc(s_hw.h, 16)
    ev_ld0 = s_hw.inc(16)
    ev_mbbc = ev_ld0

    @blk.gpsimd
    def _(g):
        g.iota(scr_p[:], pattern=[[0, 1]], base=0, channel_multiplier=1)
        g.iota(scr_i[:], pattern=[[1, 256]], base=0, channel_multiplier=-1).then_inc(s_g.h, 1)
    ev_iota = s_g.inc()

    @blk.vector
    def _(v):
        W(v, ev_iota)
        v.tensor_copy(iota_d[:], scr_i[:])
        v.memset(consts[:, 0:1], EXP_BIAS)
        v.memset(consts[:, 1:2], 10000.0)
        v.memset(consts[:, 3:4], 1e-5)
        v.memset(six_b[:], 6.0)
        v.memset(ones_bf[:], 1.0)
        v.tensor_copy(consts[:, 2:3], scr_p[:]).then_inc(s_v.h, 1)
    ev_c0 = s_v.inc()

    @blk.scalar
    def _(a):
        W(a, ev_c0)
        a.activation(lnb[:], consts[:, 1:2], AF.Ln).then_inc(s_a.h, 1)
    ev_ln = s_a.inc()

    @blk.vector
    def _(v):
        W(v, ev_ln)
        for pc in range(npair):
            v.tensor_scalar(scr_f2[:, pc:pc + 1], consts[:, 2:3], float(128 * pc), None, op0=ALU.add)
            ii = v.tensor_scalar(scr_f2[:, pc:pc + 1], scr_f2[:, pc:pc + 1],
                                 float(-1.0 / (d // 2)), lnb[:, 0:1], op0=ALU.mult, op1=ALU.mult)
        ii.then_inc(s_v.h, 1)
    ev_iv0 = s_v.inc()

    @blk.scalar
    def _(a):
        W(a, ev_iv0)
        a.activation(iv[:], scr_f2[:, 0:npair], AF.Exp).then_inc(s_a.h, 1)
    ev_iv = s_a.inc()

    es12 = ExitStack()
    wsh = [sb(f"wsh{dc}", [128, d], BF16, stack=es12) for dc in range(ndc)]
    wqT = wkT = wsh
    wvT = [sb(f"wvT{dc}", [128, d], BF16, stack=es12) for dc in range(ndc)]
    w_stage = sb("w_stage", [128, 3 * 1024], BF16, stack=es12)

    wprep_state = {"slot_free": [None, None, None], "idx": 0}

    def emit_wprep(jobs, extra_wait=None):
        for w_ap, wT, permute, ec in jobs:
            slot = wprep_state["idx"] % 3
            wprep_state["idx"] += 1
            if permute:
                j, odd = divmod(ec, 2)
                src = w_ap[256 * j + odd: 256 * j + 256: 2, :]
            else:
                src = w_ap[ec * 128:(ec + 1) * 128, :]
            prev_ev = wprep_state["slot_free"][slot]

            @blk.gpsimd
            def _(g, src=src, slot=slot, prev_ev=prev_ev):
                W(g, prev_ev)
                g.dma_start(out=w_stage[:, slot * 1024:(slot + 1) * 1024],
                            in_=src).then_inc(s_gp.h, 16)
            ev_wld = s_gp.inc(16)

            @blk.sync
            def _(sy, slot=slot, wT=wT, ec=ec, ev_wld=ev_wld, extra_wait=extra_wait):
                W(sy, ev_wld, extra_wait)
                src3 = w_stage[:, slot * 1024:(slot + 1) * 1024].rearrange("p (c t) -> p c t", t=128)
                for dc in range(ndc):
                    sy.dma_start(out=wT[dc][:, ec * 128:(ec + 1) * 128], in_=src3[:, dc, :],
                                 transpose=True).then_inc(s_hw.h, 16)
            wprep_state["slot_free"][slot] = s_hw.inc(16 * ndc)
        return s_hw.last

    ev_wT_all = emit_wprep([(wq, wsh, True, ec) for ec in range(ndc)]
                           + [(wv, wvT, False, ec) for ec in range(ndc)])


    if phases < 1:
        @blk.sync
        def _(sy):
            sy.wait_ge(s_x.h, s_x.n)
            sy.wait_ge(s_hw.h, s_hw.n)
            sy.wait_ge(s_sc.h, s_sc.n)
            sy.wait_ge(s_gp.h, s_gp.n)
        return nc

    def alloc_quant_scratch(stack, pfx="", wmax=None):
        wm = wmax or dff
        return {
            "bm": sb(pfx + "q_bm", [128, wm // 16], F32, stack=stack),
            "s2": sb(pfx + "q_s2", [128, wm // 16], F32, stack=stack),
            "rc": sb(pfx + "q_rc", [128, wm // 16], F32, stack=stack),
            "y": sb(pfx + "q_y", [128, wm], F32, stack=stack),
            "g1": sb(pfx + "q_g1", [128, wm], BF16, stack=stack),
            "g2": sb(pfx + "q_g2", [128, wm], BF16, stack=stack),
            "g3": sb(pfx + "q_g3", [128, wm], BF16, stack=stack),
            "bmr": sb(pfx + "q_bmr", [128, wm // 16], F32, stack=stack),
            "rb": sb(pfx + "q_rb", [128, wm], F32, stack=stack),
            "sb": sb(pfx + "q_sb", [128, wm], F32, stack=stack),
        }

    def emit_quant_mx(q, xin, w, out, pre=(), inc_sem=None):
        nb = w // 16

        @blk.vector
        def _(v):
            W(v, *pre)
            v.tensor_tensor(q["y"][:, 0:w], xin[:, 0:w], xin[:, 0:w],
                            op=ALU.mult).then_inc(s_v.h, 1)
            v.wait_ge(s_v.h, s_v.inc()[1])
            cur = q["y"][:, 0:w].rearrange("p (b k) -> p b k", k=16)
            off = 0
            kk = 16
            while kk > 2:
                seg = q["rb"][:, off:off + nb * (kk // 2)]
                v.tensor_tensor(seg.rearrange("p (b k) -> p b k", k=kk // 2),
                                cur[:, :, 0::2], cur[:, :, 1::2],
                                op=ALU.max).then_inc(s_v.h, 1)
                v.wait_ge(s_v.h, s_v.inc()[1])
                cur = seg.rearrange("p (b k) -> p b k", k=kk // 2)
                off += nb * (kk // 2)
                kk //= 2
            v.tensor_tensor(q["bm"][:, 0:nb], cur[:, :, 0], cur[:, :, 1],
                            op=ALU.max).then_inc(s_v.h, 1)
        ev_bm = s_v.inc()

        @blk.scalar
        def _(a):
            W(a, ev_bm)
            a.activation(q["bmr"][:, 0:nb], q["bm"][:, 0:nb], AF.Sqrt).then_inc(s_a.h, 1)
        ev_bmr = s_a.inc()

        @blk.vector
        def _(v):
            v.reciprocal(q["rc"][:, 0:nb], q["bm"][:, 0:nb]).then_inc(s_v.h, 1)
            v.wait_ge(s_v.h, s_v.inc()[1])
            v.tensor_scalar(q["rc"][:, 0:nb], q["rc"][:, 0:nb], 36.0, 1e24,
                            op0=ALU.mult, op1=ALU.min).then_inc(s_v.h, 1)
            v.wait_ge(s_v.h, s_v.inc()[1])
            v.tensor_copy(q["rb"][:, 0:w].rearrange("p (b k) -> p b k", k=16),
                          q["rc"][:, 0:nb].unsqueeze(2).broadcast_to((128, nb, 16))
                          ).then_inc(s_v.h, 1)
            v.wait_ge(s_v.h, s_v.inc()[1])
            v.tensor_tensor(q["sb"][:, 0:w], q["y"][:, 0:w], q["rb"][:, 0:w],
                            op=ALU.mult).then_inc(s_v.h, 1)
            v.wait_ge(s_v.h, s_v.inc()[1])
            zv = q["sb"][:, 0:w]
            g1, g2, g3 = q["g1"][:, 0:w], q["g2"][:, 0:w], q["g3"][:, 0:w]
            v.tensor_scalar(g1, zv, 0.0625, None, op0=ALU.is_gt)
            for mid2 in (0.5625, 1.5625, 3.0625):
                v.scalar_tensor_tensor(g1, zv, mid2, g1,
                                       op0=ALU.is_gt, op1=ALU.add).then_inc(s_v.h, 1)
                v.wait_ge(s_v.h, s_v.inc()[1])
            v.tensor_scalar(g2, zv, 6.25, 2.0, op0=ALU.is_gt, op1=ALU.mult)
            v.tensor_tensor(g3, g1, g2, op=ALU.add).then_inc(s_v.h, 1)
            v.wait_ge(s_v.h, s_v.inc()[1])
            v.tensor_scalar(g2, zv, 12.25, 2.0, op0=ALU.is_gt, op1=ALU.mult)
            v.tensor_tensor(g1, g3, g2, op=ALU.add).then_inc(s_v.h, 1)
            v.wait_ge(s_v.h, s_v.inc()[1])
            v.tensor_scalar(g2, zv, 25.0, 4.0, op0=ALU.is_gt, op1=ALU.mult)
            v.tensor_tensor(g3, g1, g2, op=ALU.add).then_inc(s_v.h, 1)
            v.wait_ge(s_v.h, s_v.inc()[1])
            W(v, ev_bmr)
            v.tensor_scalar(q["s2"][:, 0:nb], q["bmr"][:, 0:nb], float(1.0 / 12.0), 5e-13,
                            op0=ALU.mult, op1=ALU.max).then_inc(s_v.h, 1)
            v.wait_ge(s_v.h, s_v.inc()[1])
            v.tensor_copy(q["sb"][:, 0:w].rearrange("p (b k) -> p b k", k=16),
                          q["s2"][:, 0:nb].unsqueeze(2).broadcast_to((128, nb, 16))
                          ).then_inc(s_v.h, 1)
            v.wait_ge(s_v.h, s_v.inc()[1])
            v.tensor_scalar(g2, xin[:, 0:w], 0.0, 2.0, op0=ALU.is_ge, op1=ALU.mult)
            v.tensor_tensor(g1, g3, g2, op=ALU.mult).then_inc(s_v.h, 1)
            v.wait_ge(s_v.h, s_v.inc()[1])
            v.tensor_tensor(g2, g1, g3, op=ALU.subtract).then_inc(s_v.h, 1)
            v.wait_ge(s_v.h, s_v.inc()[1])
            v.tensor_tensor(out[:, 0:w], g2, q["sb"][:, 0:w],
                            op=ALU.mult).then_inc(s_v.h, 1)
        return s_v.inc()

    es_pb = ExitStack()
    qsc = alloc_quant_scratch(es_pb)
    q_in = sb("q_in", [128, dff], F32, stack=es_pb)
    q_out = sb("q_out", [128, dff], BF16, stack=es_pb)
    q_tp = sb("q_tp", [128, dff], BF16, stack=es_pb)

    ffnw_jobs = []
    for w_ap, wT_st, nrc, wwidth in ((wg, wgT_st, nicc, d), (wu, wuT_st, nicc, d),
                                     (wd, wdT_st, ndc, dff)):
        for rc_ in range(nrc):
            ffnw_jobs.append((w_ap, wT_st, rc_, wwidth))

    ev_qv = None
    ev_qtp = None
    ev_qst = None
    if p0b_jobs is not None:
        ffnw_jobs = ffnw_jobs[:p0b_jobs]
    for w_ap, wT_st, rc_, wwidth in ffnw_jobs:
        src = w_ap[rc_ * 128:(rc_ + 1) * 128, :]

        @blk.sync
        def _(sy, src=src, wwidth=wwidth, prev_tp=ev_qtp if ev_qtp else ev_wT_all):
            W(sy, prev_tp)
            sy.dma_start(out=q_in[:, 0:wwidth], in_=src).then_inc(s_x.h, 16)
        ev_qld = s_x.inc(16)

        ev_qv = emit_quant_mx(qsc, q_in, wwidth, q_out, pre=(ev_qld, ev_qtp))

        @blk.sync
        def _(sy, wwidth=wwidth, ev_qv=ev_qv, prev_st=ev_qst):
            W(sy, ev_qv, prev_st)
            sy.dma_start(out=q_tp[:, 0:wwidth].rearrange("p (c t) -> p c t", t=128),
                         in_=q_out[:, 0:wwidth].rearrange("p (c t) -> p c t", t=128),
                         transpose=True).then_inc(s_hw.h, 16)
        ev_qtp = s_hw.inc(16)

        @blk.scalar
        def _(a, wT_st=wT_st, rc_=rc_, wwidth=wwidth, ev_qtp=ev_qtp):
            W(a, ev_qtp)
            for cc in range(wwidth // 128):
                a.dma_start(out=wT_st[cc * 128:(cc + 1) * 128, rc_ * 128:(rc_ + 1) * 128],
                            in_=q_tp[:, cc * 128:(cc + 1) * 128]).then_inc(s_sc.h, 16)
        ev_qst = s_sc.inc(16 * (wwidth // 128))
    ev_ffnw_done = ev_qst


    if phases < 2:
        @blk.sync
        def _(sy):
            sy.wait_ge(s_x.h, s_x.n)
            sy.wait_ge(s_hw.h, s_hw.n)
            sy.wait_ge(s_sc.h, s_sc.n)
            sy.wait_ge(s_gp.h, s_gp.n)
        return nc
    es_pb.close()
    KT = [sb(f"KT{e}", [128, nkt * 128], BF16, stack=es12) for e in range(ndc)]
    V = sb("V", [128, nkt * 1024], BF16, stack=es12)
    es_pE = ExitStack()
    xf = [sb(f"xf{i}", [128, d], F32, stack=es_pE) for i in range(2)]
    xnb = [sb(f"xnb{i}", [128, d], BF16, stack=es_pE) for i in range(2)]
    xnT = sb("xnT", [128, ndc * 512], BF16, stack=es_pE)
    st_tmp = sb("st_tmp", [128, 16], F32, stack=es_pE)
    st_mr = sb("st_mr", [128, 2], F32, stack=es_pE)
    posb = sb("posb", [128, 512], F32, stack=es_pE)
    tb_cos = sb("tb_cos", [128, 512], BF16, stack=es_pE)
    tb_sin = sb("tb_sin", [128, 512], BF16, stack=es_pE)
    tb_phi = sb("tb_phi", [128, 512], F32, stack=es_pE)
    tb_ni = sb("tb_ni", [128, 512], I32, stack=es_pE)
    rp_t1 = sb("rp_t1", [128, 512], F32, stack=es_pE)
    rp_sin = sb("rp_sin", [128, 512], F32, stack=es_pE)
    rp_t2 = sb("rp_t2", [128, 512], F32, stack=es_pE)
    q_evac = V

    state = {
        "ln_prev_act": None,
        "sin_prev_rope": None,
        "rope_ev": {},
        "xnT_user": None,
    }

    def emit_ln_tile(xin, xout, ev_loaded, stats_dst=None):
        prev_act = state["ln_prev_act"]

        @blk.vector
        def _(v):
            W(v, ev_loaded, prev_act)
            ng = d // 512
            for gi in range(ng):
                v.bn_stats(st_tmp[:, gi * 6:(gi + 1) * 6],
                           xin[:, gi * 512:(gi + 1) * 512]).then_inc(s_v.h, 1)
            v.wait_ge(s_v.h, s_v.n + ng)
            v.bn_aggr(st_tmp[:, 12:14], st_tmp[:, 0:6 * ng]).then_inc(s_v.h, 1)
        for _gi in range(d // 512):
            s_v.inc()
        ev_bn = s_v.inc()

        @blk.scalar
        def _(a):
            W(a, ev_bn)
            a.activation(st_tmp[:, 14:15], st_tmp[:, 13:14], AF.Sqrt,
                         bias=consts[:, 3:4]).then_inc(s_a.h, 1)
        ev_sq = s_a.inc()

        @blk.vector
        def _(v):
            W(v, ev_sq, ev_bn)
            v.reciprocal(st_mr[:, 1:2], st_tmp[:, 14:15]).then_inc(s_v.h, 1)
            v.wait_ge(s_v.h, s_v.inc()[1])
            ii = v.tensor_scalar(st_mr[:, 0:1], st_tmp[:, 12:13], st_mr[:, 1:2], -1.0,
                                 op0=ALU.mult, op1=ALU.mult)
            if stats_dst is not None:
                v.tensor_scalar(stats_dst[:, 0:1], st_tmp[:, 12:13], st_mr[:, 1:2], -1.0,
                                op0=ALU.mult, op1=ALU.mult)
                ii = v.tensor_scalar(stats_dst[:, 1:2], st_mr[:, 1:2], 1.0, None, op0=ALU.mult)
            ii.then_inc(s_v.h, 1)
        ev_st = s_v.inc()

        @blk.scalar
        def _(a):
            W(a, ev_st)
            a.activation(xout[:], xin[:], AF.Identity,
                         bias=st_mr[:, 0:1], scale=st_mr[:, 1:2]).then_inc(s_a.h, 1)
        ev = s_a.inc()
        state["ln_prev_act"] = ev
        return ev

    def emit_tables(pc, ev_pos):
        prev_rope = state["sin_prev_rope"]
        pos_src = posb[:]

        @blk.vector
        def _(v):
            W(v, ev_pos, ev_iv)
            def reduce_into(dst, phi_ap):
                v.tensor_scalar(tb_ni[:], phi_ap, INV_2PI, None, op0=ALU.mult)
                v.tensor_copy(rp_t2[:], tb_ni[:])
                v.scalar_tensor_tensor(rp_t1[:], rp_t2[:], -TWO_PI_HI, phi_ap, op0=ALU.mult, op1=ALU.add)
                v.scalar_tensor_tensor(rp_t1[:], rp_t2[:], -TWO_PI_LO, rp_t1[:], op0=ALU.mult, op1=ALU.add)
                v.tensor_scalar(rp_t2[:], rp_t1[:], PI_F, 1.0, op0=ALU.is_gt)
                v.scalar_tensor_tensor(rp_t1[:], rp_t2[:], -TWO_PI_HI, rp_t1[:], op0=ALU.mult, op1=ALU.add)
                v.scalar_tensor_tensor(rp_t1[:], rp_t2[:], -TWO_PI_LO, rp_t1[:], op0=ALU.mult, op1=ALU.add)
                return v.tensor_scalar(dst, rp_t1[:], PI_F, -PI_F, op0=ALU.min, op1=ALU.max)
            v.tensor_scalar(tb_phi[:], pos_src, iv[:, pc:pc + 1], None, op0=ALU.mult)
            reduce_into(rp_sin[:], tb_phi[:])
            v.tensor_scalar(tb_phi[:], tb_phi[:], PI_F / 2, None, op0=ALU.add)
            reduce_into(rp_t2[:], tb_phi[:]).then_inc(s_v.h, 1)
        ev_red = s_v.inc()

        @blk.scalar
        def _(a):
            W(a, ev_red, prev_rope)
            a.activation(tb_sin[:], rp_sin[:], AF.Sin)
            a.activation(tb_cos[:], rp_t2[:], AF.Sin).then_inc(s_a.h, 1)
        return s_a.inc()

    def emit_qk_block(wT, dst_kind, ev_xnT, ev_pos, kt_base, extra_mm_wait=None, mode=9,
                      dbg_tab=False):
        last = None
        for pc in range(npair):
            ev_tab = emit_tables(pc, ev_pos)
            if dbg_tab and pc == 0:
                @blk.sync
                def _(sy, ev_tab=ev_tab):
                    W(sy, ev_tab)
                    sy.dma_start(out=dbg_cos, in_=tb_cos[:]).then_inc(s_hw.h, 16)
                    sy.dma_start(out=dbg_sin, in_=tb_sin[:]).then_inc(s_hw.h, 16)
                s_hw.inc(32)
            if mode < 3:
                continue
            bsl = (pc % 2) * 2
            prev_rope_ev = state["rope_ev"].get(bsl)

            @blk.tensor
            def _(t, pc=pc, bsl=bsl, prev_rope_ev=prev_rope_ev, ev_xnT=ev_xnT):
                W(t, ev_xnT, prev_rope_ev, extra_mm_wait if pc == 0 else None)
                for sub in range(2):
                    ec = 2 * pc + sub
                    for dc in range(ndc):
                        mm = t.matmul(pb[bsl + sub][:],
                                      wT[dc][:, ec * 128:(ec + 1) * 128],
                                      xnT[:, dc * 512:(dc + 1) * 512],
                                      start=(dc == 0), stop=(dc == ndc - 1))
                mm.then_inc(s_t.h, 1)
            ev_mm = s_t.inc()

            if dst_kind == 'K':
                dE = KT[2 * pc][:, kt_base * 128: kt_base * 128 + 512]
                dO = KT[2 * pc + 1][:, kt_base * 128: kt_base * 128 + 512]
            else:
                dE = q_evac[:, (2 * pc) * 512:(2 * pc) * 512 + 512]
                dO = q_evac[:, (2 * pc + 1) * 512:(2 * pc + 1) * 512 + 512]

            @blk.vector
            def _(v, bsl=bsl, ev_mm=ev_mm, ev_tab=ev_tab, dE=dE, dO=dO):
                W(v, ev_mm, ev_tab)
                if mode < 4:
                    v.nop().then_inc(s_v.h, 1)
                    return
                E, O = pb[bsl], pb[bsl + 1]
                v.tensor_tensor(rp_t1[:], E[:], tb_cos[:], op=ALU.mult)
                v.tensor_tensor(rp_t2[:], O[:], tb_sin[:], op=ALU.mult)
                v.tensor_tensor(dE, rp_t1[:], rp_t2[:], op=ALU.subtract)
                v.tensor_tensor(rp_t1[:], O[:], tb_cos[:], op=ALU.mult)
                v.tensor_tensor(rp_t2[:], E[:], tb_sin[:], op=ALU.mult)
                v.tensor_tensor(dO, rp_t1[:], rp_t2[:], op=ALU.add).then_inc(s_v.h, 1)
            ev_rope = s_v.inc()
            state["rope_ev"][bsl] = ev_rope
            state["sin_prev_rope"] = ev_rope
            last = ev_rope
        return last

    def emit_xn_block(src_ap, blk_i, stats_base, extra_tp_wait):
        ev_last_tp = None
        for ti in range(4):
            buf = ti % 2
            prev_used = state.get(f"xf_free_{buf}")

            @blk.sync
            def _(sy, ti=ti, buf=buf, prev_used=prev_used):
                W(sy, prev_used)
                sy.dma_start(out=xf[buf][:],
                             in_=src_ap[blk_i * 512 + ti * 128: blk_i * 512 + (ti + 1) * 128, :]
                             ).then_inc(s_x.h, 16)
            ev_xl = s_x.inc(16)
            sd = None
            if stats_base is not None:
                tix = stats_base + ti
                sd = statsq[:, tix * 2:tix * 2 + 2]
            ev_xn = emit_ln_tile(xf[buf], xnb[buf], ev_xl, stats_dst=sd)
            if dbg and state.get("dbg_xn_pending"):
                state["dbg_xn_pending"] = False
                @blk.sync
                def _(sy, buf=buf, ev_xn=ev_xn):
                    W(sy, ev_xn)
                    sy.dma_start(out=dbg_xn, in_=xnb[buf][:]).then_inc(s_hw.h, 16)
                s_hw.inc(16)

            @blk.sync
            def _(sy, buf=buf, ti=ti, ev_xn=ev_xn, extra=extra_tp_wait if ti == 0 else None):
                W(sy, ev_xn, extra)
                src3 = xnb[buf][:].rearrange("p (c t) -> p c t", t=128)
                dst3 = xnT.rearrange("p (c u) -> p c u", u=512)[:, :, ti * 128:(ti + 1) * 128]
                sy.dma_start(out=dst3, in_=src3, transpose=True).then_inc(s_hw.h, 16)
            ev_tp = s_hw.inc(16)
            state[f"xf_free_{buf}"] = ev_tp
            ev_last_tp = ev_tp
        return ev_last_tp

    state["xf_free_0"] = state["xf_free_1"] = ev_ffnw_done
    state["dbg_xn_pending"] = dbg

    ev_q_store = None
    for qb in range(nqblk):
        @blk.sync
        def _(sy, qb=qb, pv=state["sin_prev_rope"]):
            W(sy, pv)
            sy.dma_start(out=posb[:], in_=pos_own[:, qb * 512:(qb + 1) * 512]).then_inc(s_x.h, 16)
        ev_pos = s_x.inc(16)

        if p1b_mode >= 1:
            ev_xnT_tp = emit_xn_block(xq_own, qb, qb * 4, state["xnT_user"])
        else:
            ev_xnT_tp = None

        @blk.vector
        def _(v, ev_xnT_tp=ev_xnT_tp, ev_q_store=ev_q_store):
            W(v, ev_xnT_tp, ev_q_store)
            v.nop().then_inc(s_v.h, 1)
        ev_xnT = s_v.inc()

        if dbg and qb == 0:
            @blk.sync
            def _(sy, ev_xnT=ev_xnT):
                W(sy, ev_xnT)
                sy.dma_start(out=dbg_xnT, in_=xnT[:]).then_inc(s_hw.h, 16)
            s_hw.inc(16)
        if p1b_mode >= 2:
            emit_qk_block(wqT, 'Q', ev_xnT, ev_pos, None,
                          extra_mm_wait=ev_wT_all if qb == 0 else None,
                          mode=p1b_mode, dbg_tab=(dbg and qb == 0))
        ev_rope_all = s_v.last
        state["xnT_user"] = s_t.last

        if p1b_mode >= 5:
            @blk.scalar
            def _(a, qb=qb, ev_rope_all=ev_rope_all):
                W(a, ev_rope_all)
                for ec in range(ndc):
                    a.dma_start(out=qt_st[ec * 128:(ec + 1) * 128, qb * 512:(qb + 1) * 512],
                                in_=q_evac[:, ec * 512:(ec + 1) * 512]).then_inc(s_sc.h, 16)
            ev_q_store = s_sc.inc(16 * ndc)


    if phases < 3:
        @blk.sync
        def _(sy):
            sy.wait_ge(s_x.h, s_x.n)
            sy.wait_ge(s_hw.h, s_hw.n)
            sy.wait_ge(s_sc.h, s_sc.n)
            sy.wait_ge(s_gp.h, s_gp.n)
        return nc
    ev_wkT_all = emit_wprep([(wk, wsh, True, ec) for ec in range(ndc)],
                            extra_wait=state["xnT_user"])

    ev_v_evac = None
    for tb in range(nblkseq):
        @blk.gpsimd
        def _(g, tb=tb, pv_ev=state["sin_prev_rope"] or ev_ffnw_done):
            W(g, pv_ev)
            g.iota(tb_ni[:], pattern=[[1, 512]], base=tb * 512, channel_multiplier=0).then_inc(s_g.h, 1)
        ev_posi = s_g.inc()

        @blk.vector
        def _(v, ev_posi=ev_posi):
            W(v, ev_posi)
            v.tensor_copy(posb[:], tb_ni[:]).then_inc(s_v.h, 1)
        ev_pos = s_v.inc()

        ev_xnT_tp = emit_xn_block(x, tb, None, state["xnT_user"])

        @blk.vector
        def _(v, ev_xnT_tp=ev_xnT_tp):
            W(v, ev_xnT_tp)
            v.nop().then_inc(s_v.h, 1)
        ev_xnT = s_v.inc()

        emit_qk_block(wkT, 'K', ev_xnT, ev_pos, tb * 4)

        for ti in range(4):
            @blk.tensor
            def _(t, tb=tb, ti=ti, ev_xnT=ev_xnT, prev_vev=ev_v_evac):
                W(t, ev_xnT, prev_vev)
                for half in range(2):
                    for dc in range(ndc):
                        mm = t.matmul(pb[4 + half][:],
                                      xnT[:, dc * 512 + ti * 128: dc * 512 + (ti + 1) * 128],
                                      wvT[dc][:, half * 512:(half + 1) * 512],
                                      start=(dc == 0), stop=(dc == ndc - 1))
                mm.then_inc(s_t.h, 1)
            ev_vmm = s_t.inc()

            @blk.vector
            def _(v, tb=tb, ti=ti, ev_vmm=ev_vmm,
                  qst=ev_q_store if (tb == 0 and ti == 0) else None):
                W(v, ev_vmm, qst)
                kt = tb * 4 + ti
                v.tensor_copy(V[:, kt * 1024: kt * 1024 + 512], pb[4][:])
                v.tensor_copy(V[:, kt * 1024 + 512: kt * 1024 + 1024],
                              pb[5][:]).then_inc(s_v.h, 1)
            ev_v_evac = s_v.inc()
        state["xnT_user"] = s_t.last


    if phases < 4:
        @blk.sync
        def _(sy):
            sy.wait_ge(s_x.h, s_x.n)
            sy.wait_ge(s_hw.h, s_hw.n)
            sy.wait_ge(s_sc.h, s_sc.n)
            sy.wait_ge(s_gp.h, s_gp.n)
        return nc
    ev_p1_v = s_v.last
    ev_p1_t = s_t.last

    es_pE.close()
    es_p2 = ExitStack()
    QTs = sb("QTs", [128, ndc * 256], BF16, stack=es_p2)
    PT = sb("PT", [128, (4 * nqb) * 256], BF16, stack=es_p2)
    msk = sb("msk", [128, 4 * 256], BF16, stack=es_p2)
    att_o = [sb(f"att_o{i}", [128, 1024], BF16, stack=es_p2) for i in range(2)]
    smsk = sb("smsk", [128, 2 * 256], F32, stack=es_p2)
    rs_ = sb("rs_", [128, 2], F32, stack=es_p2)

    ev_exp_bank = {}
    ev_av_mm_prev_slot = None
    ev_attn_evac = {}
    ev_attn_store = {}
    ev_s_mm_last = None
    for slot in range(nqb):
        nktile = 4 * (slot + 1)

        @blk.sync
        def _(sy, slot=slot, ev_q_store=ev_q_store, prev_s=ev_s_mm_last):
            W(sy, ev_q_store, prev_s)
            if slot == 0:
                W(sy, ev_p1_t, ev_p1_v)
            for ec in range(ndc):
                sy.dma_start(out=QTs[:, ec * 256:(ec + 1) * 256],
                             in_=qt_st[ec * 128:(ec + 1) * 128, slot * 256:(slot + 1) * 256]
                             ).then_inc(s_x.h, 16)
        ev_qts = s_x.inc(16 * ndc)

        @blk.vector
        def _(v, slot=slot):
            W(v, ev_mbbc)
            for t4 in range(4):
                ii = v.tensor_scalar(msk[:, t4 * 256:(t4 + 1) * 256], iota_d[:],
                                     mb_bc[:, slot * 4 + t4: slot * 4 + t4 + 1], -1e9,
                                     op0=ALU.is_lt, op1=ALU.mult)
            ii.then_inc(s_v.h, 1)
        ev_msk = s_v.inc()

        ev_exp_or_mask = []
        for kt in range(nktile):
            sbk = kt % 2
            prev_exp = ev_exp_bank.get(sbk)

            @blk.tensor
            def _(t, slot=slot, kt=kt, sbk=sbk, prev_exp=prev_exp, ev_qts=ev_qts,
                  prev_ms=ev_exp_bank.get((sbk, "m"))):
                W(t, ev_qts, prev_exp, prev_ms)
                if slot == 0 and kt == 0:
                    W(t, ev_p1_v)
                for ec in range(ndc):
                    mm = t.matmul(pb[sbk][:, 0:256],
                                  KT[ec][:, kt * 128:(kt + 1) * 128],
                                  QTs[:, ec * 256:(ec + 1) * 256],
                                  start=(ec == 0), stop=(ec == ndc - 1))
                mm.then_inc(s_t.h, 1)
            ev_smm = s_t.inc()
            ev_s_mm_last = ev_smm

            if kt >= nktile - 4:
                t4 = kt - (nktile - 4)

                @blk.vector
                def _(v, kt=kt, t4=t4, sbk=sbk, ev_smm=ev_smm, ev_msk=ev_msk,
                      prev_mexp=ev_exp_bank.get((sbk, "mx"))):
                    W(v, ev_smm, ev_msk, prev_mexp)
                    v.tensor_tensor(smsk[:, sbk * 256:(sbk + 1) * 256],
                                    pb[sbk][:, 0:256],
                                    msk[:, t4 * 256:(t4 + 1) * 256],
                                    op=ALU.add).then_inc(s_v.h, 1)
                ev_ms = s_v.inc()
                ev_exp_bank[(sbk, "m")] = ev_ms

                @blk.scalar
                def _(a, kt=kt, sbk=sbk, ev_ms=ev_ms, prev_av=ev_av_mm_prev_slot):
                    W(a, ev_ms, prev_av)
                    a.activation(PT[:, kt * 256:(kt + 1) * 256],
                                 smsk[:, sbk * 256:(sbk + 1) * 256],
                                 AF.Exp, bias=consts[:, 0:1], scale=SM_SCALE).then_inc(s_a.h, 1)
                ev_exp = s_a.inc()
                ev_exp_bank[(sbk, "mx")] = ev_exp
            else:
                @blk.scalar
                def _(a, kt=kt, sbk=sbk, ev_smm=ev_smm, prev_av=ev_av_mm_prev_slot):
                    W(a, ev_smm, prev_av)
                    a.activation(PT[:, kt * 256:(kt + 1) * 256], pb[sbk][:, 0:256],
                                 AF.Exp, bias=consts[:, 0:1], scale=SM_SCALE).then_inc(s_a.h, 1)
                ev_exp = s_a.inc()
            ev_exp_bank[sbk] = ev_exp
            ev_exp_or_mask.append(ev_exp)

        ev_pt_a = s_a.last
        ev_pt_v = s_v.last

        for qq in range(2):
            bk0, bk1, bks = pb[2 + qq * 2], pb[3 + qq * 2], pb[6 + qq]
            prev_evac = ev_attn_evac.get(qq)

            @blk.tensor
            def _(t, qq=qq, bk0=bk0, bk1=bk1, bks=bks, nktile=nktile,
                  ev_pt_a=ev_pt_a, ev_pt_v=ev_pt_v, prev_evac=prev_evac):
                W(t, ev_pt_a, ev_pt_v, prev_evac)
                for kt in range(nktile):
                    lhs = PT[:, kt * 256 + qq * 128: kt * 256 + qq * 128 + 128]
                    t.matmul(bk0[:], lhs, V[:, kt * 1024: kt * 1024 + 512],
                             start=(kt == 0), stop=(kt == nktile - 1))
                    t.matmul(bk1[:], lhs, V[:, kt * 1024 + 512: kt * 1024 + 1024],
                             start=(kt == 0), stop=(kt == nktile - 1))
                    mm = t.matmul(bks[:, 0:1], lhs, ones_bf[:],
                                  start=(kt == 0), stop=(kt == nktile - 1))
                mm.then_inc(s_t.h, 1)
            ev_avmm = s_t.inc()

            @blk.vector
            def _(v, qq=qq, ev_avmm=ev_avmm, bk0=bk0, bk1=bk1, bks=bks,
                  prev_store=ev_attn_store.get(qq)):
                W(v, ev_avmm, prev_store)
                v.reciprocal(rs_[:, qq:qq + 1], bks[:, 0:1]).then_inc(s_v.h, 1)
                v.wait_ge(s_v.h, s_v.inc()[1])
                v.tensor_scalar(att_o[qq][:, 0:512], bk0[:], rs_[:, qq:qq + 1], None, op0=ALU.mult)
                v.tensor_scalar(att_o[qq][:, 512:1024], bk1[:], rs_[:, qq:qq + 1],
                                None, op0=ALU.mult).then_inc(s_v.h, 1)
            ev_attn_evac[qq] = s_v.inc()

            @blk.sync
            def _(sy, slot=slot, qq=qq, ev_ae=ev_attn_evac[qq]):
                W(sy, ev_ae)
                row = slot * 256 + qq * 128
                sy.dma_start(out=at_st[row:row + 128, :], in_=att_o[qq][:]).then_inc(s_hw.h, 16)
            ev_attn_store[qq] = s_hw.inc(16)
        ev_av_mm_prev_slot = s_t.last
    ev_attn_all = s_hw.last

    es_p2.close()
    es12.close()


    if phases < 5:
        @blk.sync
        def _(sy):
            sy.wait_ge(s_x.h, s_x.n)
            sy.wait_ge(s_hw.h, s_hw.n)
            sy.wait_ge(s_sc.h, s_sc.n)
            sy.wait_ge(s_gp.h, s_gp.n)
        return nc
    qsc3 = alloc_quant_scratch(es, pfx="p3", wmax=d)
    dbg_gate_sb = sb("dbg_gate_sb", [128, 512], F32) if dbg else None
    dbg_sil_sb = sb("dbg_sil_sb", [128, 512], F32) if dbg else None
    dbg_up_sb = sb("dbg_up_sb", [128, 512], F32) if dbg else None
    dbg_hb_sb = sb("dbg_hb_sb", [128, 512], F32) if dbg else None
    dbg_sb_sb = sb("dbg_sb_sb", [128, 512], F32) if dbg else None
    dbg_bm_sb = sb("dbg_bm_sb", [128, 64], F32) if dbg else None
    wgTs = [sb(f"wgTs{dc}", [128, dff], BF16) for dc in range(ndc)]
    wuTs = [sb(f"wuTs{dc}", [128, dff], BF16) for dc in range(ndc)]
    wdTs = sb("wdTs", [128, nicc * 1024], BF16)
    xq_f = [sb(f"xq_f{i}", [128, d], F32) for i in range(2)]
    at_t = [sb(f"at_t{i}", [128, d], BF16) for i in range(2)]
    xnq = sb("xnq", [128, d], F32)
    xqq = sb("xqq", [128, d], BF16)
    xqT = sb("xqT", [128, d], BF16)
    hsil = sb("hsil", [128, 512], F32)
    hbuf = sb("hbuf", [128, 512], F32)
    hq = sb("hq", [128, dff], BF16)
    hqT = sb("hqT", [128, dff], BF16)
    yout = sb("yout", [128, d], F32)

    @blk.scalar
    def _(a, ev_ffnw_done=ev_ffnw_done, ev_attn_all=ev_attn_all):
        W(a, ev_ffnw_done, ev_attn_all)
        for dc in range(ndc):
            a.dma_start(out=wgTs[dc][:], in_=wgT_st[dc * 128:(dc + 1) * 128, :]).then_inc(s_sc.h, 16)
            a.dma_start(out=wuTs[dc][:], in_=wuT_st[dc * 128:(dc + 1) * 128, :]).then_inc(s_sc.h, 16)
        for ic in range(nicc):
            a.dma_start(out=wdTs[:, ic * 1024:(ic + 1) * 1024],
                        in_=wdT_st[ic * 128:(ic + 1) * 128, :]).then_inc(s_sc.h, 16)
    ev_wffn = s_sc.inc(16 * (2 * ndc + nicc))

    pg, pu = pb[0], pb[1]
    pd = [pb[2], pb[3]]

    ev_tile_free = [None, None]
    ev_xqT = None
    ev_hqT = None
    ev_gumm_last = None
    ev_dmm = None
    ev_y_store_prev = None
    for ti in range(ntile_own):
        buf = ti % 2

        @blk.sync
        def _(sy, ti=ti, buf=buf, prev=ev_tile_free[buf], ev_attn_all=ev_attn_all):
            W(sy, prev, ev_attn_all if ti == 0 else None)
            sy.dma_start(out=xq_f[buf][:], in_=xq_own[ti * 128:(ti + 1) * 128, :]).then_inc(s_x.h, 16)
            sy.dma_start(out=at_t[buf][:], in_=at_st[ti * 128:(ti + 1) * 128, :]).then_inc(s_x.h, 16)
        ev_tl = s_x.inc(32)

        @blk.scalar
        def _(a, ti=ti, buf=buf, ev_tl=ev_tl, ev_sq_w=ev_q_store if ti == 0 else None,
              prev_q=ev_xqT):
            W(a, ev_tl, ev_sq_w, prev_q)
            a.activation(xnq[:], xq_f[buf][:], AF.Identity,
                         bias=statsq[:, ti * 2:ti * 2 + 1], scale=statsq[:, ti * 2 + 1:ti * 2 + 2]
                         ).then_inc(s_a.h, 1)
        ev_xn = s_a.inc()

        ev_xq = emit_quant_mx(qsc3, xnq, d, xqq, pre=(ev_xn, ev_xqT))

        @blk.sync
        def _(sy, ti=ti, ev_xq=ev_xq, ev_gumm_last=ev_gumm_last):
            W(sy, ev_xq, ev_gumm_last)
            if dbg and ti == 0:
                sy.dma_start(out=dbg_xnq, in_=xnq[:]).then_inc(s_hw.h, 16)
                sy.dma_start(out=dbg_xqq, in_=xqq[:]).then_inc(s_hw.h, 16)
                sy.dma_start(out=dbg_st[:, 0:ntile_own * 2], in_=statsq[:, 0:ntile_own * 2]).then_inc(s_hw.h, 16)
                sy.dma_start(out=dbg_xf, in_=xq_f[0][:]).then_inc(s_hw.h, 16)
            sy.dma_start(out=xqT.rearrange("p (c t) -> p c t", t=128),
                         in_=xqq.rearrange("p (c t) -> p c t", t=128), transpose=True
                         ).then_inc(s_hw.h, 16)
        if dbg and ti == 0:
            s_hw.inc(64)
        ev_xqT = s_hw.inc(16)

        ev_h_ready = None
        ev_sil_prev = None
        for ic, w in enumerate(iw):
            c0 = ic * 512

            @blk.tensor
            def _(t, ic=ic, w=w, c0=c0, ev_xqT=ev_xqT, ev_h_ready=ev_h_ready):
                if ic == 0:
                    W(t, ev_xqT, ev_wffn)
                else:
                    W(t, ev_h_ready)
                for dc in range(ndc):
                    t.matmul(pg[:, 0:w], xqT[:, dc * 128:(dc + 1) * 128],
                             wgTs[dc][:, c0:c0 + w], start=(dc == 0), stop=(dc == ndc - 1))
                for dc in range(ndc):
                    mm = t.matmul(pu[:, 0:w], xqT[:, dc * 128:(dc + 1) * 128],
                                  wuTs[dc][:, c0:c0 + w], start=(dc == 0), stop=(dc == ndc - 1))
                mm.then_inc(s_t.h, 1)
            ev_gumm = s_t.inc()
            ev_gumm_last = ev_gumm

            @blk.scalar
            def _(a, w=w, ev_gumm=ev_gumm, ev_sil_prev=ev_sil_prev):
                W(a, ev_gumm, ev_sil_prev)
                a.activation(hsil[:, 0:w], pg[:, 0:w], AF.Silu).then_inc(s_a.h, 1)
            ev_sil = s_a.inc()

            @blk.vector
            def _(v, ti=ti, ic=ic, w=w, c0=c0, ev_sil=ev_sil, prev_hqT=ev_hqT):
                W(v, ev_sil, prev_hqT if ic == 0 else None)
                if dbg and ti == 0 and ic == 0:
                    v.tensor_copy(dbg_gate_sb[:], pg[:, 0:512])
                    v.tensor_copy(dbg_sil_sb[:], hsil[:, 0:512])
                    v.tensor_copy(dbg_up_sb[:], pu[:, 0:512])
                v.tensor_tensor(hbuf[:, 0:w], hsil[:, 0:w], pu[:, 0:w],
                                op=ALU.mult).then_inc(s_v.h, 1)
            ev_hb = s_v.inc()
            if dbg and ti == 0 and ic == 0:
                @blk.vector
                def _(v, ev_hb=ev_hb):
                    W(v, ev_hb)
                    v.tensor_copy(dbg_hb_sb[:], hbuf[:, 0:512]).then_inc(s_v.h, 1)
                ev_hb = s_v.inc()
            ev_h_ready = emit_quant_mx(qsc3, hbuf, w, hq[:, c0:c0 + w], pre=(ev_hb,))
            if dbg and ti == 0 and ic == 0:
                @blk.vector
                def _(v, ev_h=ev_h_ready):
                    W(v, ev_h)
                    v.tensor_copy(dbg_sb_sb[:], qsc3["sb"][:, 0:512])
                    v.tensor_copy(dbg_bm_sb[:, 0:32], qsc3["bm"][:, 0:32])
                    ii = v.tensor_copy(dbg_bm_sb[:, 32:64], qsc3["s2"][:, 0:32])
                    ii.then_inc(s_v.h, 1)
                ev_h_ready = s_v.inc()
            ev_sil_prev = ev_h_ready

        @blk.sync
        def _(sy, ti=ti, ev_h=ev_h_ready, ev_dmm=ev_dmm):
            W(sy, ev_h, ev_dmm)
            if dbg and ti == 0:
                sy.dma_start(out=dbg_hq, in_=hq[:]).then_inc(s_hw.h, 16)
                sy.dma_start(out=dbg_gate, in_=dbg_gate_sb[:]).then_inc(s_hw.h, 16)
                sy.dma_start(out=dbg_sil, in_=dbg_sil_sb[:]).then_inc(s_hw.h, 16)
                sy.dma_start(out=dbg_up, in_=dbg_up_sb[:]).then_inc(s_hw.h, 16)
                sy.dma_start(out=dbg_hb, in_=dbg_hb_sb[:]).then_inc(s_hw.h, 16)
                sy.dma_start(out=dbg_sb2, in_=dbg_sb_sb[:]).then_inc(s_hw.h, 16)
                sy.dma_start(out=dbg_bm2, in_=dbg_bm_sb[:]).then_inc(s_hw.h, 16)
            sy.dma_start(out=hqT.rearrange("p (c t) -> p c t", t=128),
                         in_=hq.rearrange("p (c t) -> p c t", t=128), transpose=True
                         ).then_inc(s_hw.h, 16)
        if dbg and ti == 0:
            s_hw.inc(112)
        ev_hqT = s_hw.inc(16)

        @blk.tensor
        def _(t, ev_hqT=ev_hqT, ev_y_prev=ev_y_store_prev):
            W(t, ev_hqT, ev_y_prev)
            for half in range(2):
                for icc in range(nicc):
                    mm = t.matmul(pd[half][:], hqT[:, icc * 128:(icc + 1) * 128],
                                  wdTs[:, icc * 1024 + half * 512:
                                       icc * 1024 + (half + 1) * 512],
                                  start=(icc == 0), stop=(icc == nicc - 1))
            mm.then_inc(s_t.h, 1)
        ev_dmm = s_t.inc()

        @blk.vector
        def _(v, buf=buf, ev_dmm=ev_dmm, ev_tl=ev_tl, ev_y_prev=ev_y_store_prev):
            W(v, ev_dmm, ev_tl, ev_y_prev)
            for half in range(2):
                sl = slice(half * 512, (half + 1) * 512)
                v.tensor_tensor(hbuf[:, 0:512], pd[half][:], at_t[buf][:, sl], op=ALU.add)
                ii = v.tensor_tensor(yout[:, sl], hbuf[:, 0:512], xq_f[buf][:, sl], op=ALU.add)
            ii.then_inc(s_v.h, 1)
        ev_yv = s_v.inc()

        @blk.scalar
        def _(a, ti=ti, ev_yv=ev_yv):
            W(a, ev_yv)
            a.dma_start(out=y[ti * 128:(ti + 1) * 128, :], in_=yout[:]).then_inc(s_sc.h, 16)
        ev_y_store_prev = s_sc.inc(16)
        ev_tile_free[buf] = ev_y_store_prev

    @blk.sync
    def _(sy):
        sy.wait_ge(s_x.h, s_x.n)
        sy.wait_ge(s_hw.h, s_hw.n)
        sy.wait_ge(s_sc.h, s_sc.n)
        sy.wait_ge(s_gp.h, s_gp.n)

    es.close()
    return nc



def _core_layout(seq=SEQ):
    nblk = seq // 256
    A, B = halves_block_pattern(nblk)
    out = []
    for half, blocks in ((0, A), (1, B)):
        rows = np.concatenate([np.arange(256 * j, 256 * (j + 1)) for j in blocks])
        bnd = []
        for i, j in enumerate(blocks):
            nktile = 4 * (i + 1)
            for t4 in range(4):
                kt = nktile - 4 + t4
                bnd.append(128 * kt - 256 * j)
        out.append((blocks, rows, np.array(bnd, dtype=np.float32)))
    return out


def kernel(L0_attn_x, L0_attn_wq, L0_attn_wk, L0_attn_wv,
           L0_ffn_w_gate, L0_ffn_w_up, L0_ffn_w_down):
    x = np.ascontiguousarray(np.asarray(L0_attn_x, dtype=np.float32))
    wq_ = np.ascontiguousarray(np.asarray(L0_attn_wq, dtype=np.float32))
    wk_ = np.ascontiguousarray(np.asarray(L0_attn_wk, dtype=np.float32))
    wv_ = np.ascontiguousarray(np.asarray(L0_attn_wv, dtype=np.float32))
    wg_ = np.ascontiguousarray(np.asarray(L0_ffn_w_gate, dtype=np.float32))
    wu_ = np.ascontiguousarray(np.asarray(L0_ffn_w_up, dtype=np.float32))
    wd_ = np.ascontiguousarray(np.asarray(L0_ffn_w_down, dtype=np.float32))

    layouts = _core_layout(SEQ)
    nc = build_program(SEQ, DM, DFF)
    in_maps = []
    for c in range(8):
        b, h = c // 2, c % 2
        blocks, rows, bnd = layouts[h]
        xb = x[b]
        in_maps.append({
            "x": xb,
            "xq_own": np.ascontiguousarray(xb[rows]),
            "pos_own": np.ascontiguousarray(np.broadcast_to(rows.astype(np.float32)[None, :], (128, rows.size))),
            "mbnd": np.ascontiguousarray(np.broadcast_to(bnd[None, :], (128, bnd.size))),
            "wq": wq_, "wk": wk_, "wv": wv_,
            "wg": wg_, "wu": wu_, "wd": wd_,
        })
    res = run_bass_kernel_spmd(nc, in_maps, core_ids=list(range(8)))
    out = np.empty((BATCH, SEQ, DM), dtype=np.float32)
    for c in range(8):
        b, h = c // 2, c % 2
        blocks, rows, bnd = layouts[h]
        out[b][rows] = res.results[c]["y"]
    return out


# version 05
# speedup vs baseline: 1.3161x; 1.3161x over previous
import numpy as np
import ml_dtypes
from contextlib import ExitStack

import concourse.bass as bass
import concourse.mybir as mybir
from concourse.bass_utils import run_bass_kernel_spmd

F32 = mybir.dt.float32
BF16 = mybir.dt.bfloat16
I32 = mybir.dt.int32
AF = mybir.ActivationFunctionType
ALU = mybir.AluOpType

PI_F = float(np.float32(np.pi))
TWO_PI_HI = float(np.float32(2 * np.pi))
TWO_PI_LO = float(np.float64(2 * np.pi) - np.float64(np.float32(2 * np.pi)))
INV_2PI = float(1.0 / (2 * np.pi))
EXP_BIAS = -24.0
SM_SCALE = 0.03125

SEQ = 4096
DM = 1024
DFF = 2816
BATCH = 4


def halves_block_pattern(nblk):
    A = [j for j in range(nblk) if j % 4 in (0, 3)]
    B = [j for j in range(nblk) if j % 4 in (1, 2)]
    return sorted(A), sorted(B)


def build_program(seq=SEQ, d=DM, dff=DFF, phases=9, p0b_jobs=None, p1b_mode=9, dbg=False):
    nc = bass.Bass("TRN2", target_bir_lowering=False, detect_race_conditions=False)
    es = ExitStack()

    nkt = seq // 128
    own = seq // 2
    nqb = own // 256
    ntile_own = own // 128
    ndc = d // 128
    npair = ndc // 2
    nicc = dff // 128
    iw = [512] * (dff // 512) + ([dff % 512] if dff % 512 else [])
    nblkseq = seq // 512
    nqblk = own // 512

    def sb(name, shape, dt=F32, stack=None):
        return (stack or es).enter_context(nc.sbuf_tensor(name, shape, dt))

    def ps_(name, shape=(128, 512), dt=F32):
        return es.enter_context(nc.psum_tensor(name, list(shape), dt))

    class Sem:
        def __init__(self, name):
            self.h = es.enter_context(nc.semaphore(name))
            self.n = 0

        def inc(self, k=1):
            self.n += k
            return (self, self.n)

        @property
        def last(self):
            return (self, self.n)

    def W(eng, *evs):
        for ev in evs:
            if ev is not None and ev[1] > 0:
                eng.wait_ge(ev[0].h, ev[1])

    x = nc.dram_tensor("x", [seq, d], F32, kind="ExternalInput").ap()
    xq_own = nc.dram_tensor("xq_own", [own, d], F32, kind="ExternalInput").ap()
    pos_own = nc.dram_tensor("pos_own", [128, own], F32, kind="ExternalInput").ap()
    mbnd = nc.dram_tensor("mbnd", [128, nqb * 4], F32, kind="ExternalInput").ap()
    wq = nc.dram_tensor("wq", [d, d], F32, kind="ExternalInput").ap()
    wk = nc.dram_tensor("wk", [d, d], F32, kind="ExternalInput").ap()
    wv = nc.dram_tensor("wv", [d, d], F32, kind="ExternalInput").ap()
    wg = nc.dram_tensor("wg", [dff, d], F32, kind="ExternalInput").ap()
    wu = nc.dram_tensor("wu", [dff, d], F32, kind="ExternalInput").ap()
    wd = nc.dram_tensor("wd", [d, dff], F32, kind="ExternalInput").ap()
    y = nc.dram_tensor("y", [own, d], F32, kind="ExternalOutput").ap()

    knd = "ExternalOutput" if dbg else "Internal"
    if dbg:
        dbg_xn = nc.dram_tensor("dbg_xn", [128, d], BF16, kind="ExternalOutput").ap()
        dbg_cos = nc.dram_tensor("dbg_cos", [128, 512], BF16, kind="ExternalOutput").ap()
        dbg_sin = nc.dram_tensor("dbg_sin", [128, 512], BF16, kind="ExternalOutput").ap()
        dbg_xnT = nc.dram_tensor("dbg_xnT", [128, 8 * 512], BF16, kind="ExternalOutput").ap()
        dbg_xnq = nc.dram_tensor("dbg_xnq", [128, d], F32, kind="ExternalOutput").ap()
        dbg_xqq = nc.dram_tensor("dbg_xqq", [128, d], BF16, kind="ExternalOutput").ap()
        dbg_gate = nc.dram_tensor("dbg_gate", [128, 512], F32, kind="ExternalOutput").ap()
        dbg_hq = nc.dram_tensor("dbg_hq", [128, dff], BF16, kind="ExternalOutput").ap()
        dbg_st = nc.dram_tensor("dbg_st", [128, 64], F32, kind="ExternalOutput").ap()
        dbg_xf = nc.dram_tensor("dbg_xf", [128, d], F32, kind="ExternalOutput").ap()
        dbg_sil = nc.dram_tensor("dbg_sil", [128, 512], F32, kind="ExternalOutput").ap()
        dbg_up = nc.dram_tensor("dbg_up", [128, 512], F32, kind="ExternalOutput").ap()
        dbg_hb = nc.dram_tensor("dbg_hb", [128, 512], F32, kind="ExternalOutput").ap()
        dbg_sb2 = nc.dram_tensor("dbg_sb2", [128, 512], F32, kind="ExternalOutput").ap()
        dbg_bm2 = nc.dram_tensor("dbg_bm2", [128, 64], F32, kind="ExternalOutput").ap()
    qt_st = nc.dram_tensor("qt_st", [d, own], BF16, kind=knd).ap()
    at_st = nc.dram_tensor("at_st", [own, d], BF16, kind=knd).ap()
    wgT_st = nc.dram_tensor("wgT_st", [d, dff], BF16, kind=knd).ap()
    wuT_st = nc.dram_tensor("wuT_st", [d, dff], BF16, kind=knd).ap()
    wdT_st = nc.dram_tensor("wdT_st", [dff, d], BF16, kind=knd).ap()

    s_x = Sem("s_x")
    s_hw = Sem("s_hw")
    s_sc = Sem("s_sc")
    s_gp = Sem("s_gp")
    s_v = Sem("s_v")
    s_a = Sem("s_a")
    s_t = Sem("s_t")
    s_g = Sem("s_g")

    blk = es.enter_context(nc.Block())

    consts = sb("consts", [128, 8], F32)
    iv = sb("iv", [128, npair], F32)
    ones_bf = sb("ones_bf", [128, 1], BF16)
    iota_d = sb("iota_d", [128, 256], F32)
    mb_bc = sb("mb_bc", [128, nqb * 4], F32)
    statsq = sb("statsq", [128, ntile_own * 2], F32)
    six_b = sb("six_b", [128, 1], F32)

    pb = [ps_(f"pb{i}") for i in range(8)]

    scr_i = sb("scr_i", [128, 256], I32)
    scr_f2 = sb("scr_f2", [128, 8], F32)
    scr_p = sb("scr_p", [128, 1], I32)
    lnb = sb("lnb", [128, 1], F32)

    @blk.sync
    def _(sy):
        sy.dma_start(out=mb_bc[:], in_=mbnd).then_inc(s_hw.h, 16)
    ev_ld0 = s_hw.inc(16)
    ev_mbbc = ev_ld0

    @blk.gpsimd
    def _(g):
        g.iota(scr_p[:], pattern=[[0, 1]], base=0, channel_multiplier=1)
        g.iota(scr_i[:], pattern=[[1, 256]], base=0, channel_multiplier=-1).then_inc(s_g.h, 1)
    ev_iota = s_g.inc()

    @blk.vector
    def _(v):
        W(v, ev_iota)
        v.tensor_copy(iota_d[:], scr_i[:])
        v.memset(consts[:, 0:1], EXP_BIAS)
        v.memset(consts[:, 1:2], 10000.0)
        v.memset(consts[:, 3:4], 1e-5)
        v.memset(six_b[:], 6.0)
        v.memset(ones_bf[:], 1.0)
        v.tensor_copy(consts[:, 2:3], scr_p[:]).then_inc(s_v.h, 1)
    ev_c0 = s_v.inc()

    @blk.scalar
    def _(a):
        W(a, ev_c0)
        a.activation(lnb[:], consts[:, 1:2], AF.Ln).then_inc(s_a.h, 1)
    ev_ln = s_a.inc()

    @blk.vector
    def _(v):
        W(v, ev_ln)
        for pc in range(npair):
            v.tensor_scalar(scr_f2[:, pc:pc + 1], consts[:, 2:3], float(128 * pc), None, op0=ALU.add)
            ii = v.tensor_scalar(scr_f2[:, pc:pc + 1], scr_f2[:, pc:pc + 1],
                                 float(-1.0 / (d // 2)), lnb[:, 0:1], op0=ALU.mult, op1=ALU.mult)
        ii.then_inc(s_v.h, 1)
    ev_iv0 = s_v.inc()

    @blk.scalar
    def _(a):
        W(a, ev_iv0)
        a.activation(iv[:], scr_f2[:, 0:npair], AF.Exp).then_inc(s_a.h, 1)
    ev_iv = s_a.inc()

    es12 = ExitStack()
    wsh = [sb(f"wsh{dc}", [128, d], BF16, stack=es12) for dc in range(ndc)]
    wqT = wkT = wsh
    wvT = [sb(f"wvT{dc}", [128, d], BF16, stack=es12) for dc in range(ndc)]
    w_stage = sb("w_stage", [128, 3 * 1024], BF16, stack=es12)

    wprep_state = {"slot_free": [None, None, None], "idx": 0}

    def emit_wprep(jobs, extra_wait=None):
        for w_ap, wT, permute, ec in jobs:
            slot = wprep_state["idx"] % 3
            wprep_state["idx"] += 1
            if permute:
                j, odd = divmod(ec, 2)
                src = w_ap[256 * j + odd: 256 * j + 256: 2, :]
            else:
                src = w_ap[ec * 128:(ec + 1) * 128, :]
            prev_ev = wprep_state["slot_free"][slot]

            @blk.gpsimd
            def _(g, src=src, slot=slot, prev_ev=prev_ev):
                W(g, prev_ev)
                g.dma_start(out=w_stage[:, slot * 1024:(slot + 1) * 1024],
                            in_=src).then_inc(s_gp.h, 16)
            ev_wld = s_gp.inc(16)

            @blk.sync
            def _(sy, slot=slot, wT=wT, ec=ec, ev_wld=ev_wld, extra_wait=extra_wait):
                W(sy, ev_wld, extra_wait)
                src3 = w_stage[:, slot * 1024:(slot + 1) * 1024].rearrange("p (c t) -> p c t", t=128)
                for dc in range(ndc):
                    sy.dma_start(out=wT[dc][:, ec * 128:(ec + 1) * 128], in_=src3[:, dc, :],
                                 transpose=True).then_inc(s_hw.h, 16)
            wprep_state["slot_free"][slot] = s_hw.inc(16 * ndc)
        return s_hw.last

    ev_wT_all = emit_wprep([(wq, wsh, True, ec) for ec in range(ndc)]
                           + [(wv, wvT, False, ec) for ec in range(ndc)])


    if phases < 1:
        @blk.sync
        def _(sy):
            sy.wait_ge(s_x.h, s_x.n)
            sy.wait_ge(s_hw.h, s_hw.n)
            sy.wait_ge(s_sc.h, s_sc.n)
            sy.wait_ge(s_gp.h, s_gp.n)
        return nc

    def alloc_quant_scratch(stack, pfx="", wmax=None):
        wm = wmax or dff
        return {
            "bm": sb(pfx + "q_bm", [128, wm // 16], F32, stack=stack),
            "s2": sb(pfx + "q_s2", [128, wm // 16], F32, stack=stack),
            "rc": sb(pfx + "q_rc", [128, wm // 16], F32, stack=stack),
            "y": sb(pfx + "q_y", [128, wm], F32, stack=stack),
            "g1": sb(pfx + "q_g1", [128, wm], BF16, stack=stack),
            "g2": sb(pfx + "q_g2", [128, wm], BF16, stack=stack),
            "g3": sb(pfx + "q_g3", [128, wm], BF16, stack=stack),
            "bmr": sb(pfx + "q_bmr", [128, wm // 16], F32, stack=stack),
            "rb": sb(pfx + "q_rb", [128, wm], F32, stack=stack),
            "sb": sb(pfx + "q_sb", [128, wm], F32, stack=stack),
        }

    def emit_quant_mx(q, xin, w, out, pre=(), inc_sem=None):
        nb = w // 16

        @blk.vector
        def _(v):
            W(v, *pre)
            v.tensor_tensor(q["y"][:, 0:w], xin[:, 0:w], xin[:, 0:w],
                            op=ALU.mult).then_inc(s_v.h, 1)
            v.wait_ge(s_v.h, s_v.inc()[1])
            cur = q["y"][:, 0:w].rearrange("p (b k) -> p b k", k=16)
            off = 0
            kk = 16
            while kk > 2:
                seg = q["rb"][:, off:off + nb * (kk // 2)]
                v.tensor_tensor(seg.rearrange("p (b k) -> p b k", k=kk // 2),
                                cur[:, :, 0::2], cur[:, :, 1::2],
                                op=ALU.max).then_inc(s_v.h, 1)
                v.wait_ge(s_v.h, s_v.inc()[1])
                cur = seg.rearrange("p (b k) -> p b k", k=kk // 2)
                off += nb * (kk // 2)
                kk //= 2
            v.tensor_tensor(q["bm"][:, 0:nb], cur[:, :, 0], cur[:, :, 1],
                            op=ALU.max).then_inc(s_v.h, 1)
        ev_bm = s_v.inc()

        @blk.scalar
        def _(a):
            W(a, ev_bm)
            a.activation(q["bmr"][:, 0:nb], q["bm"][:, 0:nb], AF.Sqrt).then_inc(s_a.h, 1)
        ev_bmr = s_a.inc()

        @blk.vector
        def _(v):
            v.reciprocal(q["rc"][:, 0:nb], q["bm"][:, 0:nb]).then_inc(s_v.h, 1)
            v.wait_ge(s_v.h, s_v.inc()[1])
            v.tensor_scalar(q["rc"][:, 0:nb], q["rc"][:, 0:nb], 36.0, 1e24,
                            op0=ALU.mult, op1=ALU.min).then_inc(s_v.h, 1)
            v.wait_ge(s_v.h, s_v.inc()[1])
            v.tensor_copy(q["rb"][:, 0:w].rearrange("p (b k) -> p b k", k=16),
                          q["rc"][:, 0:nb].unsqueeze(2).broadcast_to((128, nb, 16))
                          ).then_inc(s_v.h, 1)
            v.wait_ge(s_v.h, s_v.inc()[1])
            v.tensor_tensor(q["sb"][:, 0:w], q["y"][:, 0:w], q["rb"][:, 0:w],
                            op=ALU.mult).then_inc(s_v.h, 1)
            v.wait_ge(s_v.h, s_v.inc()[1])
            zv = q["sb"][:, 0:w]
            g1, g2, g3 = q["g1"][:, 0:w], q["g2"][:, 0:w], q["g3"][:, 0:w]
            v.tensor_scalar(g1, zv, 0.0625, None, op0=ALU.is_gt)
            for mid2 in (0.5625, 1.5625, 3.0625):
                v.scalar_tensor_tensor(g1, zv, mid2, g1,
                                       op0=ALU.is_gt, op1=ALU.add).then_inc(s_v.h, 1)
                v.wait_ge(s_v.h, s_v.inc()[1])
            v.tensor_scalar(g2, zv, 6.25, 2.0, op0=ALU.is_gt, op1=ALU.mult)
            v.tensor_tensor(g3, g1, g2, op=ALU.add).then_inc(s_v.h, 1)
            v.wait_ge(s_v.h, s_v.inc()[1])
            v.tensor_scalar(g2, zv, 12.25, 2.0, op0=ALU.is_gt, op1=ALU.mult)
            v.tensor_tensor(g1, g3, g2, op=ALU.add).then_inc(s_v.h, 1)
            v.wait_ge(s_v.h, s_v.inc()[1])
            v.tensor_scalar(g2, zv, 25.0, 4.0, op0=ALU.is_gt, op1=ALU.mult)
            v.tensor_tensor(g3, g1, g2, op=ALU.add).then_inc(s_v.h, 1)
            v.wait_ge(s_v.h, s_v.inc()[1])
            W(v, ev_bmr)
            v.tensor_scalar(q["s2"][:, 0:nb], q["bmr"][:, 0:nb], float(1.0 / 12.0), 5e-13,
                            op0=ALU.mult, op1=ALU.max).then_inc(s_v.h, 1)
            v.wait_ge(s_v.h, s_v.inc()[1])
            v.tensor_copy(q["sb"][:, 0:w].rearrange("p (b k) -> p b k", k=16),
                          q["s2"][:, 0:nb].unsqueeze(2).broadcast_to((128, nb, 16))
                          ).then_inc(s_v.h, 1)
            v.wait_ge(s_v.h, s_v.inc()[1])
            v.tensor_scalar(g2, xin[:, 0:w], 0.0, 2.0, op0=ALU.is_ge, op1=ALU.mult)
            v.tensor_tensor(g1, g3, g2, op=ALU.mult).then_inc(s_v.h, 1)
            v.wait_ge(s_v.h, s_v.inc()[1])
            v.tensor_tensor(g2, g1, g3, op=ALU.subtract).then_inc(s_v.h, 1)
            v.wait_ge(s_v.h, s_v.inc()[1])
            v.tensor_tensor(out[:, 0:w], g2, q["sb"][:, 0:w],
                            op=ALU.mult).then_inc(s_v.h, 1)
        return s_v.inc()

    def _quant_p1_ops(q, xin, w):
        nb = w // 16
        ops = [lambda v: v.tensor_tensor(q["y"][:, 0:w], xin[:, 0:w], xin[:, 0:w],
                                         op=ALU.mult)]
        segs = []
        off = 0
        kk = 16
        while kk > 2:
            segs.append((kk, off))
            off += nb * (kk // 2)
            kk //= 2
        def src_ap(kk, off_prev):
            if kk == 16:
                return q["y"][:, 0:w].rearrange("p (b k) -> p b k", k=16)
            return q["rb"][:, off_prev:off_prev + nb * kk].rearrange("p (b k) -> p b k", k=kk)
        prev_off = [None]
        for kk, off in segs:
            def f(v, kk=kk, off=off, po=prev_off[0]):
                cur = src_ap(kk, po)
                seg = q["rb"][:, off:off + nb * (kk // 2)]
                return v.tensor_tensor(seg.rearrange("p (b k) -> p b k", k=kk // 2),
                                       cur[:, :, 0::2], cur[:, :, 1::2], op=ALU.max)
            ops.append(f)
            prev_off[0] = off
        def flast(v, po=prev_off[0]):
            cur = src_ap(2, po)
            return v.tensor_tensor(q["bm"][:, 0:nb], cur[:, :, 0], cur[:, :, 1], op=ALU.max)
        ops.append(flast)
        return ops

    def _quant_p2_ops(q, xin, w, out, bmr_ev_box):
        nb = w // 16
        g1 = q["g1"][:, 0:w]
        g2 = q["g2"][:, 0:w]
        g3 = q["g3"][:, 0:w]
        zv = q["sb"][:, 0:w]
        ops = []
        ops.append(lambda v: v.reciprocal(q["rc"][:, 0:nb], q["bm"][:, 0:nb]))
        ops.append(lambda v: v.tensor_scalar(q["rc"][:, 0:nb], q["rc"][:, 0:nb], 36.0, 1e24,
                                             op0=ALU.mult, op1=ALU.min))
        ops.append(lambda v: v.tensor_copy(
            q["rb"][:, 0:w].rearrange("p (b k) -> p b k", k=16),
            q["rc"][:, 0:nb].unsqueeze(2).broadcast_to((128, nb, 16))))
        ops.append(lambda v: v.tensor_tensor(q["sb"][:, 0:w], q["y"][:, 0:w],
                                             q["rb"][:, 0:w], op=ALU.mult))
        ops.append(lambda v: v.tensor_scalar(g1, zv, 0.0625, None, op0=ALU.is_gt))
        for mid2 in (0.5625, 1.5625, 3.0625):
            ops.append(lambda v, mid2=mid2: v.scalar_tensor_tensor(
                g1, zv, mid2, g1, op0=ALU.is_gt, op1=ALU.add))
        ops.append(lambda v: v.tensor_scalar(g2, zv, 6.25, 2.0, op0=ALU.is_gt, op1=ALU.mult))
        ops.append(lambda v: v.tensor_tensor(g3, g1, g2, op=ALU.add))
        ops.append(lambda v: v.tensor_scalar(g2, zv, 12.25, 2.0, op0=ALU.is_gt, op1=ALU.mult))
        ops.append(lambda v: v.tensor_tensor(g1, g3, g2, op=ALU.add))
        ops.append(lambda v: v.tensor_scalar(g2, zv, 25.0, 4.0, op0=ALU.is_gt, op1=ALU.mult))
        ops.append(lambda v: v.tensor_tensor(g3, g1, g2, op=ALU.add))
        def s2f(v):
            W(v, bmr_ev_box[0])
            return v.tensor_scalar(q["s2"][:, 0:nb], q["bmr"][:, 0:nb],
                                   float(1.0 / 12.0), 5e-13, op0=ALU.mult, op1=ALU.max)
        ops.append(s2f)
        ops.append(lambda v: v.tensor_copy(
            q["sb"][:, 0:w].rearrange("p (b k) -> p b k", k=16),
            q["s2"][:, 0:nb].unsqueeze(2).broadcast_to((128, nb, 16))))
        ops.append(lambda v: v.tensor_scalar(g2, xin[:, 0:w], 0.0, 2.0,
                                             op0=ALU.is_ge, op1=ALU.mult))
        ops.append(lambda v: v.tensor_tensor(g1, g3, g2, op=ALU.mult))
        ops.append(lambda v: v.tensor_tensor(g2, g1, g3, op=ALU.subtract))
        ops.append(lambda v: v.tensor_tensor(out[:, 0:w], g2, q["sb"][:, 0:w], op=ALU.mult))
        return ops

    def emit_quant_group(streams):
        n = len(streams)
        assert n >= 2
        p1 = [_quant_p1_ops(q, xin, w) for (q, xin, w, out, pre) in streams]
        bmr_boxes = [[None] for _ in range(n)]
        p2 = [_quant_p2_ops(q, xin, w, out, bmr_boxes[i])
              for i, (q, xin, w, out, pre) in enumerate(streams)]
        ev_bm = [None] * n
        ev_fin = [None] * n

        @blk.vector
        def _(v):
            for i in range(n):
                W(v, *streams[i][4])
            kmax = max(len(x) for x in p1)
            for k in range(kmax):
                for i in range(n):
                    if k < len(p1[i]):
                        ins = p1[i][k](v)
                        if k == len(p1[i]) - 1:
                            ins.then_inc(s_v.h, 1)
        for i in range(n):
            ev_bm[i] = s_v.inc()

        @blk.scalar
        def _(a):
            for i, (q, xin, w, out, pre) in enumerate(streams):
                nb = w // 16
                W(a, ev_bm[i])
                a.activation(q["bmr"][:, 0:nb], q["bm"][:, 0:nb],
                             AF.Sqrt).then_inc(s_a.h, 1)
        for i in range(n):
            bmr_boxes[i][0] = s_a.inc()

        @blk.vector
        def _(v):
            kmax = max(len(x) for x in p2)
            for k in range(kmax):
                for i in range(n):
                    if k < len(p2[i]):
                        ins = p2[i][k](v)
                        if k == len(p2[i]) - 1:
                            ins.then_inc(s_v.h, 1)
        for i in range(n):
            ev_fin[i] = s_v.inc()
        return ev_fin

    es_pb = ExitStack()
    qscA = alloc_quant_scratch(es_pb, pfx="A")
    qscB = alloc_quant_scratch(es_pb, pfx="B")
    q_in = [sb(f"q_in{i}", [128, dff], F32, stack=es_pb) for i in range(2)]
    q_out = [sb(f"q_out{i}", [128, dff], BF16, stack=es_pb) for i in range(2)]
    q_tp = [sb(f"q_tp{i}", [128, dff], BF16, stack=es_pb) for i in range(2)]

    ffnw_jobs = []
    for w_ap, wT_st, nrc, wwidth in ((wg, wgT_st, nicc, d), (wu, wuT_st, nicc, d),
                                     (wd, wdT_st, ndc, dff)):
        for rc_ in range(nrc):
            ffnw_jobs.append((w_ap, wT_st, rc_, wwidth))

    ev_qtp = [None, None]
    ev_qst = [None, None]
    ev_qld_last = None
    if p0b_jobs is not None:
        ffnw_jobs = ffnw_jobs[:p0b_jobs]
    for j0 in range(0, len(ffnw_jobs), 2):
        pair = ffnw_jobs[j0:j0 + 2]
        ev_qld = []
        for i, (w_ap, wT_st, rc_, wwidth) in enumerate(pair):
            srcap = w_ap[rc_ * 128:(rc_ + 1) * 128, :]

            @blk.sync
            def _(sy, srcap=srcap, i=i, wwidth=wwidth,
                  gate=ev_qtp[i] if ev_qtp[i] else ev_wT_all):
                W(sy, gate)
                sy.dma_start(out=q_in[i][:, 0:wwidth], in_=srcap).then_inc(s_x.h, 16)
            ev_qld.append(s_x.inc(16))
        if len(pair) == 2:
            evs = emit_quant_group([
                (qscA, q_in[0], pair[0][3], q_out[0], (ev_qld[0], ev_qtp[0])),
                (qscB, q_in[1], pair[1][3], q_out[1], (ev_qld[1], ev_qtp[1])),
            ])
        else:
            evs = [emit_quant_mx(qscA, q_in[0], pair[0][3], q_out[0],
                                 pre=(ev_qld[0], ev_qtp[0]))]
        for i, (w_ap, wT_st, rc_, wwidth) in enumerate(pair):
            @blk.sync
            def _(sy, i=i, wwidth=wwidth, ev_qv=evs[i], prev_st=ev_qst[i]):
                W(sy, ev_qv, prev_st)
                sy.dma_start(out=q_tp[i][:, 0:wwidth].rearrange("p (c t) -> p c t", t=128),
                             in_=q_out[i][:, 0:wwidth].rearrange("p (c t) -> p c t", t=128),
                             transpose=True).then_inc(s_hw.h, 16)
            ev_qtp[i] = s_hw.inc(16)

            @blk.scalar
            def _(a2, i=i, wT_st=wT_st, rc_=rc_, wwidth=wwidth, ev_tp=ev_qtp[i]):
                W(a2, ev_tp)
                for cc in range(wwidth // 128):
                    a2.dma_start(out=wT_st[cc * 128:(cc + 1) * 128, rc_ * 128:(rc_ + 1) * 128],
                                 in_=q_tp[i][:, cc * 128:(cc + 1) * 128]).then_inc(s_sc.h, 16)
            ev_qst[i] = s_sc.inc(16 * (wwidth // 128))
    _c = [e for e in ev_qst if e]
    ev_ffnw_done = max(_c, key=lambda t: t[1]) if _c else None

    es_pb.close()
    KT = [sb(f"KT{e}", [128, nkt * 128], BF16, stack=es12) for e in range(ndc)]
    V = sb("V", [128, nkt * 1024], BF16, stack=es12)
    es_pE = ExitStack()
    xf = [sb(f"xf{i}", [128, d], F32, stack=es_pE) for i in range(2)]
    xnb = [sb(f"xnb{i}", [128, d], BF16, stack=es_pE) for i in range(2)]
    xnT = sb("xnT", [128, ndc * 512], BF16, stack=es_pE)
    st_tmp = sb("st_tmp", [128, 16], F32, stack=es_pE)
    st_mr = sb("st_mr", [128, 2], F32, stack=es_pE)
    posb = sb("posb", [128, 512], F32, stack=es_pE)
    tb_cos = sb("tb_cos", [128, 512], BF16, stack=es_pE)
    tb_sin = sb("tb_sin", [128, 512], BF16, stack=es_pE)
    tb_phi = sb("tb_phi", [128, 512], F32, stack=es_pE)
    tb_ni = sb("tb_ni", [128, 512], I32, stack=es_pE)
    rp_t1 = sb("rp_t1", [128, 512], F32, stack=es_pE)
    rp_sin = sb("rp_sin", [128, 512], F32, stack=es_pE)
    rp_t2 = sb("rp_t2", [128, 512], F32, stack=es_pE)
    q_evac = V

    state = {
        "ln_prev_act": None,
        "sin_prev_rope": None,
        "rope_ev": {},
        "xnT_user": None,
    }

    def emit_ln_tile(xin, xout, ev_loaded, stats_dst=None):
        prev_act = state["ln_prev_act"]

        @blk.vector
        def _(v):
            W(v, ev_loaded, prev_act)
            ng = d // 512
            for gi in range(ng):
                v.bn_stats(st_tmp[:, gi * 6:(gi + 1) * 6],
                           xin[:, gi * 512:(gi + 1) * 512]).then_inc(s_v.h, 1)
            v.wait_ge(s_v.h, s_v.n + ng)
            v.bn_aggr(st_tmp[:, 12:14], st_tmp[:, 0:6 * ng]).then_inc(s_v.h, 1)
        for _gi in range(d // 512):
            s_v.inc()
        ev_bn = s_v.inc()

        @blk.scalar
        def _(a):
            W(a, ev_bn)
            a.activation(st_tmp[:, 14:15], st_tmp[:, 13:14], AF.Sqrt,
                         bias=consts[:, 3:4]).then_inc(s_a.h, 1)
        ev_sq = s_a.inc()

        @blk.vector
        def _(v):
            W(v, ev_sq, ev_bn)
            v.reciprocal(st_mr[:, 1:2], st_tmp[:, 14:15]).then_inc(s_v.h, 1)
            v.wait_ge(s_v.h, s_v.inc()[1])
            ii = v.tensor_scalar(st_mr[:, 0:1], st_tmp[:, 12:13], st_mr[:, 1:2], -1.0,
                                 op0=ALU.mult, op1=ALU.mult)
            if stats_dst is not None:
                v.tensor_scalar(stats_dst[:, 0:1], st_tmp[:, 12:13], st_mr[:, 1:2], -1.0,
                                op0=ALU.mult, op1=ALU.mult)
                ii = v.tensor_scalar(stats_dst[:, 1:2], st_mr[:, 1:2], 1.0, None, op0=ALU.mult)
            ii.then_inc(s_v.h, 1)
        ev_st = s_v.inc()

        @blk.scalar
        def _(a):
            W(a, ev_st)
            a.activation(xout[:], xin[:], AF.Identity,
                         bias=st_mr[:, 0:1], scale=st_mr[:, 1:2]).then_inc(s_a.h, 1)
        ev = s_a.inc()
        state["ln_prev_act"] = ev
        return ev

    def emit_tables(pc, ev_pos):
        prev_rope = state["sin_prev_rope"]
        pos_src = posb[:]

        @blk.vector
        def _(v):
            W(v, ev_pos, ev_iv)
            def reduce_into(dst, phi_ap):
                v.tensor_scalar(tb_ni[:], phi_ap, INV_2PI, None, op0=ALU.mult)
                v.tensor_copy(rp_t2[:], tb_ni[:])
                v.scalar_tensor_tensor(rp_t1[:], rp_t2[:], -TWO_PI_HI, phi_ap, op0=ALU.mult, op1=ALU.add)
                v.scalar_tensor_tensor(rp_t1[:], rp_t2[:], -TWO_PI_LO, rp_t1[:], op0=ALU.mult, op1=ALU.add)
                v.tensor_scalar(rp_t2[:], rp_t1[:], PI_F, 1.0, op0=ALU.is_gt)
                v.scalar_tensor_tensor(rp_t1[:], rp_t2[:], -TWO_PI_HI, rp_t1[:], op0=ALU.mult, op1=ALU.add)
                v.scalar_tensor_tensor(rp_t1[:], rp_t2[:], -TWO_PI_LO, rp_t1[:], op0=ALU.mult, op1=ALU.add)
                return v.tensor_scalar(dst, rp_t1[:], PI_F, -PI_F, op0=ALU.min, op1=ALU.max)
            v.tensor_scalar(tb_phi[:], pos_src, iv[:, pc:pc + 1], None, op0=ALU.mult)
            reduce_into(rp_sin[:], tb_phi[:])
            v.tensor_scalar(tb_phi[:], tb_phi[:], PI_F / 2, None, op0=ALU.add)
            reduce_into(rp_t2[:], tb_phi[:]).then_inc(s_v.h, 1)
        ev_red = s_v.inc()

        @blk.scalar
        def _(a):
            W(a, ev_red, prev_rope)
            a.activation(tb_sin[:], rp_sin[:], AF.Sin)
            a.activation(tb_cos[:], rp_t2[:], AF.Sin).then_inc(s_a.h, 1)
        return s_a.inc()

    def emit_qk_block(wT, dst_kind, ev_xnT, ev_pos, kt_base, extra_mm_wait=None, mode=9,
                      dbg_tab=False):
        last = None
        for pc in range(npair):
            ev_tab = emit_tables(pc, ev_pos)
            if dbg_tab and pc == 0:
                @blk.sync
                def _(sy, ev_tab=ev_tab):
                    W(sy, ev_tab)
                    sy.dma_start(out=dbg_cos, in_=tb_cos[:]).then_inc(s_hw.h, 16)
                    sy.dma_start(out=dbg_sin, in_=tb_sin[:]).then_inc(s_hw.h, 16)
                s_hw.inc(32)
            if mode < 3:
                continue
            bsl = (pc % 2) * 2
            prev_rope_ev = state["rope_ev"].get(bsl)

            @blk.tensor
            def _(t, pc=pc, bsl=bsl, prev_rope_ev=prev_rope_ev, ev_xnT=ev_xnT):
                W(t, ev_xnT, prev_rope_ev, extra_mm_wait if pc == 0 else None)
                for sub in range(2):
                    ec = 2 * pc + sub
                    for dc in range(ndc):
                        mm = t.matmul(pb[bsl + sub][:],
                                      wT[dc][:, ec * 128:(ec + 1) * 128],
                                      xnT[:, dc * 512:(dc + 1) * 512],
                                      start=(dc == 0), stop=(dc == ndc - 1))
                mm.then_inc(s_t.h, 1)
            ev_mm = s_t.inc()

            if dst_kind == 'K':
                dE = KT[2 * pc][:, kt_base * 128: kt_base * 128 + 512]
                dO = KT[2 * pc + 1][:, kt_base * 128: kt_base * 128 + 512]
            else:
                dE = q_evac[:, (2 * pc) * 512:(2 * pc) * 512 + 512]
                dO = q_evac[:, (2 * pc + 1) * 512:(2 * pc + 1) * 512 + 512]

            @blk.vector
            def _(v, bsl=bsl, ev_mm=ev_mm, ev_tab=ev_tab, dE=dE, dO=dO):
                W(v, ev_mm, ev_tab)
                if mode < 4:
                    v.nop().then_inc(s_v.h, 1)
                    return
                E, O = pb[bsl], pb[bsl + 1]
                v.tensor_tensor(rp_t1[:], E[:], tb_cos[:], op=ALU.mult)
                v.tensor_tensor(rp_t2[:], O[:], tb_sin[:], op=ALU.mult)
                v.tensor_tensor(dE, rp_t1[:], rp_t2[:], op=ALU.subtract)
                v.tensor_tensor(rp_t1[:], O[:], tb_cos[:], op=ALU.mult)
                v.tensor_tensor(rp_t2[:], E[:], tb_sin[:], op=ALU.mult)
                v.tensor_tensor(dO, rp_t1[:], rp_t2[:], op=ALU.add).then_inc(s_v.h, 1)
            ev_rope = s_v.inc()
            state["rope_ev"][bsl] = ev_rope
            state["sin_prev_rope"] = ev_rope
            last = ev_rope
        return last

    def emit_xn_block(src_ap, blk_i, stats_base, extra_tp_wait):
        ev_last_tp = None
        for ti in range(4):
            buf = ti % 2
            prev_used = state.get(f"xf_free_{buf}")

            @blk.sync
            def _(sy, ti=ti, buf=buf, prev_used=prev_used):
                W(sy, prev_used)
                sy.dma_start(out=xf[buf][:],
                             in_=src_ap[blk_i * 512 + ti * 128: blk_i * 512 + (ti + 1) * 128, :]
                             ).then_inc(s_x.h, 16)
            ev_xl = s_x.inc(16)
            sd = None
            if stats_base is not None:
                tix = stats_base + ti
                sd = statsq[:, tix * 2:tix * 2 + 2]
            ev_xn = emit_ln_tile(xf[buf], xnb[buf], ev_xl, stats_dst=sd)
            if dbg and state.get("dbg_xn_pending"):
                state["dbg_xn_pending"] = False
                @blk.sync
                def _(sy, buf=buf, ev_xn=ev_xn):
                    W(sy, ev_xn)
                    sy.dma_start(out=dbg_xn, in_=xnb[buf][:]).then_inc(s_hw.h, 16)
                s_hw.inc(16)

            @blk.sync
            def _(sy, buf=buf, ti=ti, ev_xn=ev_xn, extra=extra_tp_wait if ti == 0 else None):
                W(sy, ev_xn, extra)
                src3 = xnb[buf][:].rearrange("p (c t) -> p c t", t=128)
                dst3 = xnT.rearrange("p (c u) -> p c u", u=512)[:, :, ti * 128:(ti + 1) * 128]
                sy.dma_start(out=dst3, in_=src3, transpose=True).then_inc(s_hw.h, 16)
            ev_tp = s_hw.inc(16)
            state[f"xf_free_{buf}"] = ev_tp
            ev_last_tp = ev_tp
        return ev_last_tp

    state["xf_free_0"] = state["xf_free_1"] = ev_ffnw_done
    state["dbg_xn_pending"] = dbg

    ev_q_store = None
    for qb in range(nqblk):
        @blk.sync
        def _(sy, qb=qb, pv=state["sin_prev_rope"],
              fence=ev_ffnw_done if qb == 0 else None):
            W(sy, pv, fence)
            sy.dma_start(out=posb[:], in_=pos_own[:, qb * 512:(qb + 1) * 512]).then_inc(s_x.h, 16)
        ev_pos = s_x.inc(16)

        if p1b_mode >= 1:
            ev_xnT_tp = emit_xn_block(xq_own, qb, qb * 4, state["xnT_user"])
        else:
            ev_xnT_tp = None

        @blk.vector
        def _(v, ev_xnT_tp=ev_xnT_tp, ev_q_store=ev_q_store):
            W(v, ev_xnT_tp, ev_q_store)
            v.nop().then_inc(s_v.h, 1)
        ev_xnT = s_v.inc()

        if dbg and qb == 0:
            @blk.sync
            def _(sy, ev_xnT=ev_xnT):
                W(sy, ev_xnT)
                sy.dma_start(out=dbg_xnT, in_=xnT[:]).then_inc(s_hw.h, 16)
            s_hw.inc(16)
        if p1b_mode >= 2:
            emit_qk_block(wqT, 'Q', ev_xnT, ev_pos, None,
                          extra_mm_wait=ev_wT_all if qb == 0 else None,
                          mode=p1b_mode, dbg_tab=(dbg and qb == 0))
        ev_rope_all = s_v.last
        state["xnT_user"] = s_t.last

        if p1b_mode >= 5:
            @blk.scalar
            def _(a, qb=qb, ev_rope_all=ev_rope_all):
                W(a, ev_rope_all)
                for ec in range(ndc):
                    a.dma_start(out=qt_st[ec * 128:(ec + 1) * 128, qb * 512:(qb + 1) * 512],
                                in_=q_evac[:, ec * 512:(ec + 1) * 512]).then_inc(s_sc.h, 16)
            ev_q_store = s_sc.inc(16 * ndc)


    if phases < 3:
        @blk.sync
        def _(sy):
            sy.wait_ge(s_x.h, s_x.n)
            sy.wait_ge(s_hw.h, s_hw.n)
            sy.wait_ge(s_sc.h, s_sc.n)
            sy.wait_ge(s_gp.h, s_gp.n)
        return nc
    ev_wkT_all = emit_wprep([(wk, wsh, True, ec) for ec in range(ndc)],
                            extra_wait=state["xnT_user"])

    ev_v_evac = None
    for tb in range(nblkseq):
        @blk.gpsimd
        def _(g, tb=tb, pv_ev=state["sin_prev_rope"] or ev_ffnw_done):
            W(g, pv_ev)
            g.iota(tb_ni[:], pattern=[[1, 512]], base=tb * 512, channel_multiplier=0).then_inc(s_g.h, 1)
        ev_posi = s_g.inc()

        @blk.vector
        def _(v, ev_posi=ev_posi):
            W(v, ev_posi)
            v.tensor_copy(posb[:], tb_ni[:]).then_inc(s_v.h, 1)
        ev_pos = s_v.inc()

        ev_xnT_tp = emit_xn_block(x, tb, None, state["xnT_user"])

        @blk.vector
        def _(v, ev_xnT_tp=ev_xnT_tp):
            W(v, ev_xnT_tp)
            v.nop().then_inc(s_v.h, 1)
        ev_xnT = s_v.inc()

        emit_qk_block(wkT, 'K', ev_xnT, ev_pos, tb * 4)

        for ti in range(4):
            @blk.tensor
            def _(t, tb=tb, ti=ti, ev_xnT=ev_xnT, prev_vev=ev_v_evac):
                W(t, ev_xnT, prev_vev)
                for half in range(2):
                    for dc in range(ndc):
                        mm = t.matmul(pb[4 + half][:],
                                      xnT[:, dc * 512 + ti * 128: dc * 512 + (ti + 1) * 128],
                                      wvT[dc][:, half * 512:(half + 1) * 512],
                                      start=(dc == 0), stop=(dc == ndc - 1))
                mm.then_inc(s_t.h, 1)
            ev_vmm = s_t.inc()

            @blk.vector
            def _(v, tb=tb, ti=ti, ev_vmm=ev_vmm,
                  qst=ev_q_store if (tb == 0 and ti == 0) else None):
                W(v, ev_vmm, qst)
                kt = tb * 4 + ti
                v.tensor_copy(V[:, kt * 1024: kt * 1024 + 512], pb[4][:])
                v.tensor_copy(V[:, kt * 1024 + 512: kt * 1024 + 1024],
                              pb[5][:]).then_inc(s_v.h, 1)
            ev_v_evac = s_v.inc()
        state["xnT_user"] = s_t.last


    if phases < 4:
        @blk.sync
        def _(sy):
            sy.wait_ge(s_x.h, s_x.n)
            sy.wait_ge(s_hw.h, s_hw.n)
            sy.wait_ge(s_sc.h, s_sc.n)
            sy.wait_ge(s_gp.h, s_gp.n)
        return nc
    ev_p1_v = s_v.last
    ev_p1_t = s_t.last

    es_pE.close()
    es_p2 = ExitStack()
    QTs = sb("QTs", [128, ndc * 256], BF16, stack=es_p2)
    PT = sb("PT", [128, (4 * nqb) * 256], BF16, stack=es_p2)
    msk = sb("msk", [128, 4 * 256], BF16, stack=es_p2)
    att_o = [sb(f"att_o{i}", [128, 1024], BF16, stack=es_p2) for i in range(2)]
    smsk = sb("smsk", [128, 2 * 256], F32, stack=es_p2)
    rs_ = sb("rs_", [128, 2], F32, stack=es_p2)

    ev_exp_bank = {}
    ev_av_mm_prev_slot = None
    ev_attn_evac = {}
    ev_attn_store = {}
    ev_s_mm_last = None
    for slot in range(nqb):
        nktile = 4 * (slot + 1)

        @blk.sync
        def _(sy, slot=slot, ev_q_store=ev_q_store, prev_s=ev_s_mm_last):
            W(sy, ev_q_store, prev_s)
            if slot == 0:
                W(sy, ev_p1_t, ev_p1_v)
            for ec in range(ndc):
                sy.dma_start(out=QTs[:, ec * 256:(ec + 1) * 256],
                             in_=qt_st[ec * 128:(ec + 1) * 128, slot * 256:(slot + 1) * 256]
                             ).then_inc(s_x.h, 16)
        ev_qts = s_x.inc(16 * ndc)

        @blk.vector
        def _(v, slot=slot):
            W(v, ev_mbbc)
            for t4 in range(4):
                ii = v.tensor_scalar(msk[:, t4 * 256:(t4 + 1) * 256], iota_d[:],
                                     mb_bc[:, slot * 4 + t4: slot * 4 + t4 + 1], -1e9,
                                     op0=ALU.is_lt, op1=ALU.mult)
            ii.then_inc(s_v.h, 1)
        ev_msk = s_v.inc()

        ev_exp_or_mask = []
        for kt in range(nktile):
            sbk = kt % 2
            prev_exp = ev_exp_bank.get(sbk)

            @blk.tensor
            def _(t, slot=slot, kt=kt, sbk=sbk, prev_exp=prev_exp, ev_qts=ev_qts,
                  prev_ms=ev_exp_bank.get((sbk, "m"))):
                W(t, ev_qts, prev_exp, prev_ms)
                if slot == 0 and kt == 0:
                    W(t, ev_p1_v)
                for ec in range(ndc):
                    mm = t.matmul(pb[sbk][:, 0:256],
                                  KT[ec][:, kt * 128:(kt + 1) * 128],
                                  QTs[:, ec * 256:(ec + 1) * 256],
                                  start=(ec == 0), stop=(ec == ndc - 1))
                mm.then_inc(s_t.h, 1)
            ev_smm = s_t.inc()
            ev_s_mm_last = ev_smm

            if kt >= nktile - 4:
                t4 = kt - (nktile - 4)

                @blk.vector
                def _(v, kt=kt, t4=t4, sbk=sbk, ev_smm=ev_smm, ev_msk=ev_msk,
                      prev_mexp=ev_exp_bank.get((sbk, "mx"))):
                    W(v, ev_smm, ev_msk, prev_mexp)
                    v.tensor_tensor(smsk[:, sbk * 256:(sbk + 1) * 256],
                                    pb[sbk][:, 0:256],
                                    msk[:, t4 * 256:(t4 + 1) * 256],
                                    op=ALU.add).then_inc(s_v.h, 1)
                ev_ms = s_v.inc()
                ev_exp_bank[(sbk, "m")] = ev_ms

                @blk.scalar
                def _(a, kt=kt, sbk=sbk, ev_ms=ev_ms, prev_av=ev_av_mm_prev_slot):
                    W(a, ev_ms, prev_av)
                    a.activation(PT[:, kt * 256:(kt + 1) * 256],
                                 smsk[:, sbk * 256:(sbk + 1) * 256],
                                 AF.Exp, bias=consts[:, 0:1], scale=SM_SCALE).then_inc(s_a.h, 1)
                ev_exp = s_a.inc()
                ev_exp_bank[(sbk, "mx")] = ev_exp
            else:
                @blk.scalar
                def _(a, kt=kt, sbk=sbk, ev_smm=ev_smm, prev_av=ev_av_mm_prev_slot):
                    W(a, ev_smm, prev_av)
                    a.activation(PT[:, kt * 256:(kt + 1) * 256], pb[sbk][:, 0:256],
                                 AF.Exp, bias=consts[:, 0:1], scale=SM_SCALE).then_inc(s_a.h, 1)
                ev_exp = s_a.inc()
            ev_exp_bank[sbk] = ev_exp
            ev_exp_or_mask.append(ev_exp)

        ev_pt_a = s_a.last
        ev_pt_v = s_v.last

        for qq in range(2):
            bk0, bk1, bks = pb[2 + qq * 2], pb[3 + qq * 2], pb[6 + qq]
            prev_evac = ev_attn_evac.get(qq)

            @blk.tensor
            def _(t, qq=qq, bk0=bk0, bk1=bk1, bks=bks, nktile=nktile,
                  ev_pt_a=ev_pt_a, ev_pt_v=ev_pt_v, prev_evac=prev_evac):
                W(t, ev_pt_a, ev_pt_v, prev_evac)
                for kt in range(nktile):
                    lhs = PT[:, kt * 256 + qq * 128: kt * 256 + qq * 128 + 128]
                    t.matmul(bk0[:], lhs, V[:, kt * 1024: kt * 1024 + 512],
                             start=(kt == 0), stop=(kt == nktile - 1))
                    t.matmul(bk1[:], lhs, V[:, kt * 1024 + 512: kt * 1024 + 1024],
                             start=(kt == 0), stop=(kt == nktile - 1))
                    mm = t.matmul(bks[:, 0:1], lhs, ones_bf[:],
                                  start=(kt == 0), stop=(kt == nktile - 1))
                mm.then_inc(s_t.h, 1)
            ev_avmm = s_t.inc()

            @blk.vector
            def _(v, qq=qq, ev_avmm=ev_avmm, bk0=bk0, bk1=bk1, bks=bks,
                  prev_store=ev_attn_store.get(qq)):
                W(v, ev_avmm, prev_store)
                v.reciprocal(rs_[:, qq:qq + 1], bks[:, 0:1]).then_inc(s_v.h, 1)
                v.wait_ge(s_v.h, s_v.inc()[1])
                v.tensor_scalar(att_o[qq][:, 0:512], bk0[:], rs_[:, qq:qq + 1], None, op0=ALU.mult)
                v.tensor_scalar(att_o[qq][:, 512:1024], bk1[:], rs_[:, qq:qq + 1],
                                None, op0=ALU.mult).then_inc(s_v.h, 1)
            ev_attn_evac[qq] = s_v.inc()

            @blk.sync
            def _(sy, slot=slot, qq=qq, ev_ae=ev_attn_evac[qq]):
                W(sy, ev_ae)
                row = slot * 256 + qq * 128
                sy.dma_start(out=at_st[row:row + 128, :], in_=att_o[qq][:]).then_inc(s_hw.h, 16)
            ev_attn_store[qq] = s_hw.inc(16)
        ev_av_mm_prev_slot = s_t.last
    ev_attn_all = s_hw.last

    es_p2.close()
    es12.close()


    if phases < 5:
        @blk.sync
        def _(sy):
            sy.wait_ge(s_x.h, s_x.n)
            sy.wait_ge(s_hw.h, s_hw.n)
            sy.wait_ge(s_sc.h, s_sc.n)
            sy.wait_ge(s_gp.h, s_gp.n)
        return nc
    wgTs = [sb(f"wgTs{dc}", [128, dff], BF16) for dc in range(ndc)]
    wuTs = [sb(f"wuTs{dc}", [128, dff], BF16) for dc in range(ndc)]
    wdTs = sb("wdTs", [128, nicc * 1024], BF16)
    xq_f = [sb(f"xq_f{i}", [128, d], F32) for i in range(2)]
    at_t = [sb(f"at_t{i}", [128, d], BF16) for i in range(2)]
    xnq = sb("xnq", [128, d], F32)
    xqq = sb("xqq", [128, d], BF16)
    xqT = sb("xqT", [128, d], BF16)
    hsil = sb("hsil", [128, 512], F32)
    hbuf = sb("hbuf", [128, 512], F32)
    hq = sb("hq", [128, dff], BF16)
    hqT = sb("hqT", [128, dff], BF16)
    yout = sb("yout", [128, d], F32)

    @blk.scalar
    def _(a, ev_ffnw_done=ev_ffnw_done, ev_attn_all=ev_attn_all):
        W(a, ev_ffnw_done, ev_attn_all)
        for dc in range(ndc):
            a.dma_start(out=wgTs[dc][:], in_=wgT_st[dc * 128:(dc + 1) * 128, :]).then_inc(s_sc.h, 16)
            a.dma_start(out=wuTs[dc][:], in_=wuT_st[dc * 128:(dc + 1) * 128, :]).then_inc(s_sc.h, 16)
        for ic in range(nicc):
            a.dma_start(out=wdTs[:, ic * 1024:(ic + 1) * 1024],
                        in_=wdT_st[ic * 128:(ic + 1) * 128, :]).then_inc(s_sc.h, 16)
    ev_wffn = s_sc.inc(16 * (2 * ndc + nicc))

    pgu = [pb[0], pb[1], pb[2], pb[3]]
    pd = [pb[4], pb[5]]

    qscA3 = alloc_quant_scratch(es, pfx="A3", wmax=512)
    qscB3 = alloc_quant_scratch(es, pfx="B3", wmax=512)
    qscX3 = alloc_quant_scratch(es, pfx="X3", wmax=512)

    hsil2 = [hsil, sb("hsil2", [128, 512], F32)]
    hbuf2 = [hbuf, sb("hbuf2", [128, 512], F32)]

    ev_tile_free = [None, None]
    ev_xqT = None
    ev_hqT = None
    ev_gumm_last = None
    ev_dmm = None
    ev_y_store_prev = None
    ev_xq_t = [None] * (ntile_own + 1)
    ev_tl_t = [None] * (ntile_own + 1)

    def emit_p3_loads(ti):
        buf = ti % 2

        @blk.sync
        def _(sy, ti=ti, buf=buf, prev=ev_tile_free[buf]):
            W(sy, prev, ev_attn_all if ti == 0 else None)
            sy.dma_start(out=xq_f[buf][:], in_=xq_own[ti * 128:(ti + 1) * 128, :]).then_inc(s_x.h, 16)
            sy.dma_start(out=at_t[buf][:], in_=at_st[ti * 128:(ti + 1) * 128, :]).then_inc(s_x.h, 16)
        ev_tl_t[ti] = s_x.inc(32)

    def emit_p3_xnact(ti):
        buf = ti % 2

        @blk.scalar
        def _(a, ti=ti, buf=buf, ev_tl=ev_tl_t[ti],
              ev_sq_w=ev_q_store if ti == 0 else None,
              prev_q=ev_xq_t[ti - 1] if ti > 0 else None):
            W(a, ev_tl, ev_sq_w, prev_q)
            a.activation(xnq[:], xq_f[buf][:], AF.Identity,
                         bias=statsq[:, ti * 2:ti * 2 + 1],
                         scale=statsq[:, ti * 2 + 1:ti * 2 + 2]).then_inc(s_a.h, 1)
        return s_a.inc()

    emit_p3_loads(0)
    ev_xn0 = emit_p3_xnact(0)
    evs0 = emit_quant_group([
        (qscA3, xnq[:, 0:512], 512, xqq[:, 0:512], (ev_xn0,)),
        (qscB3, xnq[:, 512:1024], 512, xqq[:, 512:1024], (ev_xn0,)),
    ])
    ev_xq_t[0] = max(evs0, key=lambda t: t[1])

    for ti in range(ntile_own):
        buf = ti % 2

        @blk.sync
        def _(sy, ti=ti, ev_xq=ev_xq_t[ti], ev_gumm_last=ev_gumm_last):
            W(sy, ev_xq, ev_gumm_last)
            sy.dma_start(out=xqT.rearrange("p (c t) -> p c t", t=128),
                         in_=xqq.rearrange("p (c t) -> p c t", t=128),
                         transpose=True).then_inc(s_hw.h, 16)
        ev_xqT = s_hw.inc(16)

        ev_hfin = [None] * 6
        ev_pair_h = [None, None]
        ev_xn_n = None
        for p in range(3):
            c0, c1 = 2 * p, 2 * p + 1
            w0 = iw[c0]
            w1 = iw[c1]

            @blk.tensor
            def _(t, p=p, w0=w0, w1=w1, c0=c0, c1=c1, ev_xqT=ev_xqT,
                  prevA=ev_pair_h[0], prevB=ev_pair_h[1],
                  prev_hq=ev_hqT if p == 0 else None):
                if p == 0:
                    W(t, ev_xqT, ev_wffn)
                W(t, prevA, prevB, prev_hq)
                for dc in range(ndc):
                    t.matmul(pgu[0][:, 0:w0], xqT[:, dc * 128:(dc + 1) * 128],
                             wgTs[dc][:, c0 * 512:c0 * 512 + w0],
                             start=(dc == 0), stop=(dc == ndc - 1))
                for dc in range(ndc):
                    mm = t.matmul(pgu[1][:, 0:w0], xqT[:, dc * 128:(dc + 1) * 128],
                                  wuTs[dc][:, c0 * 512:c0 * 512 + w0],
                                  start=(dc == 0), stop=(dc == ndc - 1))
                mm.then_inc(s_t.h, 1)
                for dc in range(ndc):
                    t.matmul(pgu[2][:, 0:w1], xqT[:, dc * 128:(dc + 1) * 128],
                             wgTs[dc][:, c1 * 512:c1 * 512 + w1],
                             start=(dc == 0), stop=(dc == ndc - 1))
                for dc in range(ndc):
                    mm = t.matmul(pgu[3][:, 0:w1], xqT[:, dc * 128:(dc + 1) * 128],
                                  wuTs[dc][:, c1 * 512:c1 * 512 + w1],
                                  start=(dc == 0), stop=(dc == ndc - 1))
                mm.then_inc(s_t.h, 1)
            ev_mm0 = s_t.inc()
            ev_mm1 = s_t.inc()
            ev_gumm_last = ev_mm1

            @blk.scalar
            def _(a, p=p, w0=w0, w1=w1, ev_mm0=ev_mm0, ev_mm1=ev_mm1,
                  pv=ev_pair_h):
                W(a, ev_mm0, pv[0])
                a.activation(hsil2[0][:, 0:w0], pgu[0][:, 0:w0], AF.Silu).then_inc(s_a.h, 1)
                W(a, ev_mm1, pv[1])
                a.activation(hsil2[1][:, 0:w1], pgu[2][:, 0:w1], AF.Silu).then_inc(s_a.h, 1)
            ev_sil0 = s_a.inc()
            ev_sil1 = s_a.inc()

            @blk.vector
            def _(v, w0=w0, w1=w1, ev_sil0=ev_sil0, ev_sil1=ev_sil1):
                W(v, ev_sil0)
                v.tensor_tensor(hbuf2[0][:, 0:w0], hsil2[0][:, 0:w0], pgu[1][:, 0:w0],
                                op=ALU.mult).then_inc(s_v.h, 1)
                W(v, ev_sil1)
                v.tensor_tensor(hbuf2[1][:, 0:w1], hsil2[1][:, 0:w1], pgu[3][:, 0:w1],
                                op=ALU.mult).then_inc(s_v.h, 1)
            ev_hb0 = s_v.inc()
            ev_hb1 = s_v.inc()
            ev_pair_h = [ev_hb0, ev_hb1]

            streams = [
                (qscA3, hbuf2[0], w0, hq[:, c0 * 512:c0 * 512 + w0],
                 (ev_hb0, ev_hqT if p == 0 else None)),
                (qscB3, hbuf2[1], w1, hq[:, c1 * 512:c1 * 512 + w1], (ev_hb1,)),
            ]
            if p >= 1 and ti + 1 < ntile_own:
                if p == 1:
                    emit_p3_loads(ti + 1)
                    ev_xn_n = emit_p3_xnact(ti + 1)
                lo = 0 if p == 1 else 512
                streams.append((qscX3, xnq[:, lo:lo + 512], 512,
                                xqq[:, lo:lo + 512], (ev_xn_n,)))
            evs = emit_quant_group(streams)
            ev_hfin[c0] = evs[0]
            ev_hfin[c1] = evs[1]
            if len(evs) == 3 and p == 2:
                ev_xq_t[ti + 1] = evs[2]

        ev_h_all = max((e for e in ev_hfin if e), key=lambda t: t[1])

        @blk.sync
        def _(sy, ev_h=ev_h_all, ev_dmm=ev_dmm):
            W(sy, ev_h, ev_dmm)
            sy.dma_start(out=hqT.rearrange("p (c t) -> p c t", t=128),
                         in_=hq.rearrange("p (c t) -> p c t", t=128), transpose=True
                         ).then_inc(s_hw.h, 16)
        ev_hqT = s_hw.inc(16)

        @blk.tensor
        def _(t, ev_hqT=ev_hqT, ev_y_prev=ev_y_store_prev):
            W(t, ev_hqT, ev_y_prev)
            for half in range(2):
                for icc in range(nicc):
                    mm = t.matmul(pd[half][:], hqT[:, icc * 128:(icc + 1) * 128],
                                  wdTs[:, icc * 1024 + half * 512:
                                       icc * 1024 + (half + 1) * 512],
                                  start=(icc == 0), stop=(icc == nicc - 1))
            mm.then_inc(s_t.h, 1)
        ev_dmm = s_t.inc()

        @blk.vector
        def _(v, buf=buf, ev_dmm=ev_dmm, ev_tl=ev_tl_t[ti], ev_y_prev=ev_y_store_prev):
            W(v, ev_dmm, ev_tl, ev_y_prev)
            for half in range(2):
                sl = slice(half * 512, (half + 1) * 512)
                v.tensor_tensor(hbuf2[0][:, 0:512], pd[half][:], at_t[buf][:, sl], op=ALU.add)
                ii = v.tensor_tensor(yout[:, sl], hbuf2[0][:, 0:512], xq_f[buf][:, sl],
                                     op=ALU.add)
            ii.then_inc(s_v.h, 1)
        ev_yv = s_v.inc()

        @blk.scalar
        def _(a, ti=ti, ev_yv=ev_yv):
            W(a, ev_yv)
            a.dma_start(out=y[ti * 128:(ti + 1) * 128, :], in_=yout[:]).then_inc(s_sc.h, 16)
        ev_y_store_prev = s_sc.inc(16)
        ev_tile_free[buf] = ev_y_store_prev

    @blk.sync
    def _(sy):
        sy.wait_ge(s_x.h, s_x.n)
        sy.wait_ge(s_hw.h, s_hw.n)
        sy.wait_ge(s_sc.h, s_sc.n)
        sy.wait_ge(s_gp.h, s_gp.n)

    es.close()
    return nc



def _core_layout(seq=SEQ):
    nblk = seq // 256
    A, B = halves_block_pattern(nblk)
    out = []
    for half, blocks in ((0, A), (1, B)):
        rows = np.concatenate([np.arange(256 * j, 256 * (j + 1)) for j in blocks])
        bnd = []
        for i, j in enumerate(blocks):
            nktile = 4 * (i + 1)
            for t4 in range(4):
                kt = nktile - 4 + t4
                bnd.append(128 * kt - 256 * j)
        out.append((blocks, rows, np.array(bnd, dtype=np.float32)))
    return out


def kernel(L0_attn_x, L0_attn_wq, L0_attn_wk, L0_attn_wv,
           L0_ffn_w_gate, L0_ffn_w_up, L0_ffn_w_down):
    x = np.ascontiguousarray(np.asarray(L0_attn_x, dtype=np.float32))
    wq_ = np.ascontiguousarray(np.asarray(L0_attn_wq, dtype=np.float32))
    wk_ = np.ascontiguousarray(np.asarray(L0_attn_wk, dtype=np.float32))
    wv_ = np.ascontiguousarray(np.asarray(L0_attn_wv, dtype=np.float32))
    wg_ = np.ascontiguousarray(np.asarray(L0_ffn_w_gate, dtype=np.float32))
    wu_ = np.ascontiguousarray(np.asarray(L0_ffn_w_up, dtype=np.float32))
    wd_ = np.ascontiguousarray(np.asarray(L0_ffn_w_down, dtype=np.float32))

    layouts = _core_layout(SEQ)
    nc = build_program(SEQ, DM, DFF)
    in_maps = []
    for c in range(8):
        b, h = c // 2, c % 2
        blocks, rows, bnd = layouts[h]
        xb = x[b]
        in_maps.append({
            "x": xb,
            "xq_own": np.ascontiguousarray(xb[rows]),
            "pos_own": np.ascontiguousarray(np.broadcast_to(rows.astype(np.float32)[None, :], (128, rows.size))),
            "mbnd": np.ascontiguousarray(np.broadcast_to(bnd[None, :], (128, bnd.size))),
            "wq": wq_, "wk": wk_, "wv": wv_,
            "wg": wg_, "wu": wu_, "wd": wd_,
        })
    res = run_bass_kernel_spmd(nc, in_maps, core_ids=list(range(8)))
    out = np.empty((BATCH, SEQ, DM), dtype=np.float32)
    for c in range(8):
        b, h = c // 2, c % 2
        blocks, rows, bnd = layouts[h]
        out[b][rows] = res.results[c]["y"]
    return out
